# version 13
# baseline (speedup 1.0000x reference)
import math
from contextlib import ExitStack

import numpy as np
import ml_dtypes

import concourse.bass as bass
import concourse.mybir as mybir
from concourse.bass_utils import run_bass_kernel_spmd

F32 = mybir.dt.float32
BF16 = mybir.dt.bfloat16
AF = mybir.ActivationFunctionType
ALU = mybir.AluOpType
AX = mybir.AxisListType

T = 2048
D = 4096
HD = 128
NH = 32
NKV = 8
LH = 32
LAT = 1024
HF = 11008
HC = HF // 128
NQC = LAT // 128
YF = HF // 4
YC = 22
DC = D // 128
NT = T // 128
N4 = T // 512
SCALE = 1.0 / math.sqrt(HD)
EPS = 1e-5
NEGINF = -1.0e30

ENGS = ("t", "v", "s", "g", "y")
DEBUG_SLOW = False


class Sem:
    def __init__(self, nc, st, name):
        self.h = st.enter_context(nc.semaphore(name))
        self.t = 0


class Prog:
    def __init__(self):
        self.q = {e: [] for e in ENGS}
        self.chain = {}

    def w(self, e, sem, val=None):
        if val is None:
            val = sem.t
        if val > 0:
            self.q[e].append(("w", sem, val))

    def wg(self, e, guards):
        for (sem, val) in guards:
            self.w(e, sem, val)

    def o(self, e, method, *args, inc=None, cinc=False, cwait=False, **kw):
        if cwait and e in self.chain:
            self.w(e, self.chain[e])
        incs = []
        if inc is not None:
            if isinstance(inc, tuple):
                inc = [inc]
            for (sem, k) in inc:
                sem.t += k
                incs.append((sem, k))
        if cinc and e in self.chain:
            cs = self.chain[e]
            if not any(s is cs for (s, _) in incs):
                assert not incs, "one sem update per instruction"
                cs.t += 1
                incs.append((cs, 1))
        self.q[e].append(("o", method, args, kw, incs))

    def replay(self, e, h):
        for it in self.q[e]:
            if it[0] == "w":
                h.wait_ge(it[1].h, it[2])
            else:
                _, m, args, kw, incs = it
                inst = getattr(h, m)(*args, **kw)
                for (sem, k) in incs:
                    inst = inst.then_inc(sem.h, k)


class Slot:

    def __init__(self, n):
        self.g = [[] for _ in range(n)]
        self.n = n

    def wait(self, p, e, i):
        p.wg(e, self.g[i % self.n])

    def set(self, i, *sems):
        self.g[i % self.n] = [(s, s.t) for s in sems]


def cdiv(a, b):
    return (a + b - 1) // b


class Ring:

    def __init__(self, S, name, n):
        self.sems = [S(name + str(i)) for i in range(n)]
        self.n = n

    def s(self, i):
        return self.sems[i % self.n]


def build_core(dbg=frozenset(), phases="ABCDEF"):
    nc = bass.Bass("TRN2", target_bir_lowering=False)
    st = ExitStack()
    p = Prog()

    def dram(name, shape, dtype, kind=None):
        if kind is None and name in dbg:
            kind = "ExternalOutput"
        if kind is None:
            h = nc.dram_tensor(name, shape, dtype)
        else:
            h = nc.dram_tensor(name, shape, dtype, kind=kind)
        return h.ap()

    x_in = dram("x", [T, D], F32, kind="ExternalInput")
    wq_d = dram("wq", [D, D], F32, kind="ExternalInput")
    wk_d = dram("wk", [D, LAT], F32, kind="ExternalInput")
    wv_d = dram("wv", [D, LAT], F32, kind="ExternalInput")
    wlq_d = dram("wlq", [D, LAT], F32, kind="ExternalInput")
    wlk_d = dram("wlk", [LAT, D], F32, kind="ExternalInput")
    wlv_d = dram("wlv", [D, D], F32, kind="ExternalInput")
    wolo_d = dram("wolo", [2 * D, D], F32, kind="ExternalInput")
    w1t_d = dram("w1t", [LAT, HF], F32, kind="ExternalInput")
    w3t_d = dram("w3t", [LAT, HF], F32, kind="ExternalInput")
    w2t_d = dram("w2t", [YF, D], F32, kind="ExternalInput")
    a13t_d = dram("a13t", [D, 4], F32, kind="ExternalInput")
    b1t_d = dram("b1t", [2, HF], F32, kind="ExternalInput")
    b3t_d = dram("b3t", [2, HF], F32, kind="ExternalInput")
    a2r_d = dram("a2r", [128, 2 * 88], F32, kind="ExternalInput")
    b2t_d = dram("b2t", [2, D], F32, kind="ExternalInput")
    c_eyeb = dram("c_eyeb", [128, 128], BF16, kind="ExternalInput")
    c_ones = dram("c_ones", [128, 129], BF16, kind="ExternalInput")
    c_eyef = dram("c_eyef", [128, 128], F32, kind="ExternalInput")
    c_mask = dram("c_mask", [128, 128], F32, kind="ExternalInput")
    c_masku = dram("c_masku", [128, 128], BF16, kind="ExternalInput")
    c_negall = dram("c_negall", [128, 128], BF16, kind="ExternalInput")
    c_rope = dram("c_rope", [128, 4 * T], BF16, kind="ExternalInput")

    QT = dram("QT", [NH * 128, T], BF16)
    KT = dram("KT", [NKV * 128, T], BF16)
    LKT = dram("LKT", [LH * 128, T], BF16)
    LQT = dram("LQT", [LAT, T], BF16)
    VT = dram("VT", [T, NKV * 128], BF16)
    LVT = dram("LVT", [T, LH * 128], BF16)
    AOT = dram("AOT", [2 * D, T], BF16)
    AOP = dram("AOP", [2, D, T], F32)
    X2 = dram("X2", [T, D], F32)
    LN2T = dram("LN2T", [D, T], BF16)
    GU = dram("GU", [HF, T], BF16)
    y_out = dram("y", [T, D], F32, kind="ExternalOutput")

    sems = {}

    def S(name):
        s = Sem(nc, st, name)
        sems[name] = s
        return s

    dW = S("dW"); dX = S("dX"); dH = S("dH"); dS = S("dS"); dR = S("dR")
    dO = S("dO"); po = S("po"); pe = S("pe"); ve = S("ve"); ac = S("ac")
    cS = S("cS"); cM = S("cM"); cE = S("cE"); cN = S("cN")
    cT = S("cT"); cTeV = S("cTeV"); cTeS = S("cTeS")
    cPV = S("cPV"); cOe = S("cOe")
    p.chain = {"v": ve, "s": ac, "g": po}
    dC = S("dC")
    dC2 = S("dC2")
    dP = S("dP")
    rxA = Ring(S, "rxA", 2)
    rwB = Ring(S, "rwB", 2)
    rsB = Ring(S, "rsB", 2)
    rsV = Ring(S, "rsV", 2)
    rq = Ring(S, "rq", 4)
    rk = Ring(S, "rk", 2)
    rl = Ring(S, "rl", 4)
    rsO = Ring(S, "rsO", 4)
    rwD = Ring(S, "rwD", 2)
    rsA1 = Ring(S, "rsA1", 2)
    rpD = Ring(S, "rpD", 2)
    rxD = Ring(S, "rxD", 2)
    rsX2 = Ring(S, "rsX2", 2)
    rsL = Ring(S, "rsL", 2)
    rlE = Ring(S, "rlE", 2)
    rwE = Ring(S, "rwE", 2)
    rsG = Ring(S, "rsG", 2)
    rgF = Ring(S, "rgF", 3)
    rwF = Ring(S, "rwF", 2)
    rxF = Ring(S, "rxF", 2)
    rsY = Ring(S, "rsY", 2)

    def barrier():
        for e in ENGS:
            for s in sems.values():
                p.w(e, s)

    eyeb = st.enter_context(nc.sbuf_tensor("eyeb", [128, 128], BF16))
    eyef = st.enter_context(nc.sbuf_tensor("eyef", [128, 128], F32))
    maskt = st.enter_context(nc.sbuf_tensor("maskt", [128, 128], F32))
    onesb = st.enter_context(nc.sbuf_tensor("onesb", [128, 129], BF16))
    onesf = st.enter_context(nc.sbuf_tensor("onesf", [128, 1], F32))
    maskub = st.enter_context(nc.sbuf_tensor("maskub", [128, 128], BF16))
    negalb = st.enter_context(nc.sbuf_tensor("negalb", [128, 128], BF16))
    epsv = st.enter_context(nc.sbuf_tensor("epsv", [128, 1], F32))
    stt = st.enter_context(nc.sbuf_tensor("stt", [128, 8], F32))
    sumsq = st.enter_context(nc.sbuf_tensor("sumsq", [128, 1], F32))

    p.o("g", "dma_start", eyeb[:], c_eyeb, inc=(dC, 16))
    p.o("g", "dma_start", eyef[:], c_eyef, inc=(dC, 16))
    p.o("g", "dma_start", maskt[:], c_mask, inc=(dC, 16))
    p.o("g", "dma_start", onesb[:], c_ones, inc=(dC, 16))
    p.o("g", "memset", onesf[:], 1.0, inc=(po, 1))
    p.o("g", "dma_start", maskub[:], c_masku, inc=(dC, 16))
    p.o("g", "dma_start", negalb[:], c_negall, inc=(dC, 16))
    p.o("g", "memset", epsv[:], EPS, inc=(po, 1))

    def ln_stats_apply(xf32, out_ap, junk, extra_out=None):
        p.w("v", ac)
        p.o("v", "tensor_reduce", stt[:, 0:1], xf32, AX.X, ALU.add, negate=True,
            cwait=True, cinc=True)
        p.o("v", "tensor_scalar", stt[:, 0:1], stt[:, 0:1], 1.0 / D, None,
            ALU.mult, inc=(ve, 1), cwait=True, cinc=True)
        p.w("s", ve)
        p.o("s", "activation", junk, xf32, AF.Square, accum_out=sumsq[:],
            inc=(ac, 1))
        p.w("v", ac)
        p.o("v", "tensor_tensor", stt[:, 1:2], stt[:, 0:1], stt[:, 0:1], ALU.mult,
            cwait=True, cinc=True)
        p.o("v", "tensor_scalar", stt[:, 2:3], sumsq[:], 1.0 / D, stt[:, 1:2],
            ALU.mult, ALU.subtract, inc=(ve, 1), cwait=True, cinc=True)
        p.w("s", ve)
        p.o("s", "activation", stt[:, 3:4], stt[:, 2:3], AF.Sqrt, bias=epsv[:],
            inc=(ac, 1), cwait=True, cinc=True)
        p.w("v", ac)
        p.o("v", "reciprocal", stt[:, 4:5], stt[:, 3:4], cwait=True, cinc=True)
        p.o("v", "tensor_tensor", stt[:, 5:6], stt[:, 0:1], stt[:, 4:5], ALU.mult,
            inc=(ve, 1), cwait=True, cinc=True)
        p.w("s", ve)
        p.o("s", "activation", out_ap, xf32, AF.Identity, bias=stt[:, 5:6],
            scale=stt[:, 4:5], inc=(ac, 1))
        if extra_out is not None:
            p.o("s", "activation", extra_out, xf32, AF.Identity, bias=stt[:, 5:6],
                scale=stt[:, 4:5], inc=(ac, 1))

    stAB = ExitStack()
    LN1T = stAB.enter_context(nc.sbuf_tensor("LN1T", [128, DC * T], BF16))

    if "A" in phases:
        stA = ExitStack()
        xa = stA.enter_context(nc.sbuf_tensor("xa", [128, 2 * D], F32))
        lnt = stA.enter_context(nc.sbuf_tensor("lnt", [128, D], BF16))
        junkA = stA.enter_context(nc.sbuf_tensor("junkA", [128, D], BF16))
        psA = [stA.enter_context(nc.psum_tensor(f"psA{i}", [128, 1024], BF16))
               for i in range(2)]
        xa_sl = Slot(2)
        psA_sl = Slot(2)

        for e in ENGS:
            p.w(e, dC)
            p.w(e, po)

        for i in range(NT):
            xa_sl.wait(p, "y", i)
            xav = xa[:, (i % 2) * D:(i % 2 + 1) * D]
            p.o("y", "dma_start", xav, x_in[128 * i:128 * (i + 1), :],
                inc=(rxA.s(i), 16))
            p.w("v", rxA.s(i))
            p.w("s", rxA.s(i))
            ln_stats_apply(xav, lnt[:], junkA[:])
            xa_sl.set(i, ac, ve)
            p.w("t", ac)
            for g in range(4):
                par = g % 2
                psA_sl.wait(p, "t", g + 4 * i)
                for j in range(8):
                    c = 8 * g + j
                    p.o("t", "transpose", psA[par][:, 128 * j:128 * (j + 1)],
                        lnt[:, 128 * c:128 * (c + 1)], eyeb[:],
                        inc=(pe, 1) if j == 7 else None)
                p.w("v", pe)
                seg = LN1T[:, 8 * g * T:8 * (g + 1) * T]
                seg3 = seg.rearrange("p (c t) -> p c t", c=8)
                p.o("v", "tensor_copy", seg3[:, :, 128 * i:128 * (i + 1)],
                    psA[par].rearrange("p (c t) -> p c t", c=8), inc=(ve, 1))
                psA_sl.set(g + 4 * i, ve)
            if DEBUG_SLOW:
                barrier()
        barrier()
        if "ln1t_dbg" in dbg and "B" not in phases:
            ln1t_dbg = dram("ln1t_dbg", [D, T], BF16, kind="ExternalOutput")
            for c in range(DC):
                p.o("y", "dma_start", ln1t_dbg[128 * c:128 * (c + 1), :],
                    LN1T[:, T * c:T * (c + 1)], inc=(dS, 16))
            barrier()
        stA.close()

    if "B" in phases:
        stB = ExitStack()
        bufs = {}

        def alloc_b(stk, with_vt):
            bufs["ropes"] = stk.enter_context(
                nc.sbuf_tensor("ropes" + str(with_vt), [128, 4 * T], BF16))
            bufs["wfm"] = stk.enter_context(
                nc.sbuf_tensor("wfm" + str(with_vt), [128, 2 * DC * 128], BF16))
            bufs["hbr"] = stk.enter_context(
                nc.sbuf_tensor("hbr" + str(with_vt), [128, T], BF16))
            bufs["hbs"] = stk.enter_context(
                nc.sbuf_tensor("hbs" + str(with_vt), [128, T], BF16))
            bufs["hbo"] = stk.enter_context(
                nc.sbuf_tensor("hbo" + str(with_vt), [128, 2 * T], BF16))
            bufs["psB"] = [stk.enter_context(
                nc.psum_tensor(f"psB{with_vt}{i}", [128, 512], F32))
                for i in range(2)]
            if with_vt:
                bufs["vtt"] = stk.enter_context(
                    nc.sbuf_tensor("vtt", [128, 2 * 512], BF16))
                bufs["psBT"] = [stk.enter_context(
                    nc.psum_tensor(f"psBT{i}", [128, 512], BF16))
                    for i in range(2)]
            p.o("g", "dma_start", bufs["ropes"][:], c_rope, inc=(dC2, 16))

        alloc_b(stB, 1)
        wfm_sl = Slot(2)
        psB_sl = Slot(2)
        hbr_sl = Slot(1)
        hbs_sl = Slot(1)
        hbo_sl = Slot(2)
        vtt_sl = Slot(2)
        psBT_sl = Slot(2)

        def emit_fm_chunk(wsrc, m, kc, mmi, rhs_tile):
            wfm, hbr, psB = bufs["wfm"], bufs["hbr"], bufs["psB"]
            par = mmi % 2
            wfm_sl.wait(p, "g", mmi)
            wsl = wsrc[:, 128 * m:128 * (m + 1)]
            wdst = wfm[:, par * DC * 128:par * DC * 128 + kc * 128]
            p.o("g", "dma_start", wdst.rearrange("p (k m) -> p k m", k=kc),
                wsl.rearrange("(k p) m -> p k m", p=128), inc=(rwB.s(mmi), 16))
            p.w("t", rwB.s(mmi))
            p.w("t", dC2)
            hbr_sl.wait(p, "s", 0)
            for n in range(N4):
                psB_sl.wait(p, "t", n)
                for k in range(kc):
                    p.o("t", "matmul", psB[n % 2][:],
                        wfm[:, par * DC * 128 + 128 * k:
                            par * DC * 128 + 128 * (k + 1)],
                        rhs_tile(k, n),
                        start=(k == 0), stop=(k == kc - 1),
                        inc=(pe, 1) if k == kc - 1 else None)
                p.w("s", pe)
                p.o("s", "copy", hbr[:, 512 * n:512 * (n + 1)], psB[n % 2][:],
                    inc=(ac, 1))
                psB_sl.set(n, ac)
            wfm_sl.set(mmi, pe)

        def emit_rope_store(m, ci, dest, mmi):
            hbr, hbs, hbo, ropes = (bufs["hbr"], bufs["hbs"], bufs["hbo"],
                                    bufs["ropes"])
            hbs_sl.wait(p, "y", 0)
            p.w("y", ac)
            p.o("y", "dma_start", hbs[0:64, :], hbr[64:128, :], inc=(dR, 16))
            p.o("y", "dma_start", hbs[64:128, :], hbr[0:64, :], inc=(dR, 16))
            p.w("v", dR)
            p.w("v", ac)
            p.w("v", dC2)
            hbo_sl.wait(p, "v", mmi)
            ob = hbo[:, (mmi % 2) * T:(mmi % 2 + 1) * T]
            p.o("v", "tensor_tensor", ob, hbr[:], ropes[:, ci * T:(ci + 1) * T],
                ALU.mult, cwait=True, cinc=True)
            p.o("v", "tensor_tensor", hbs[:], hbs[:],
                ropes[:, (ci + 1) * T:(ci + 2) * T], ALU.mult, cinc=True)
            p.o("v", "tensor_tensor", ob, ob, hbs[:], ALU.add, inc=(ve, 1),
                cwait=True, cinc=True)
            hbr_sl.set(0, ve)
            hbs_sl.set(0, ve)
            p.w("y", ve)
            p.o("y", "dma_start", dest[128 * m:128 * (m + 1), :], ob,
                inc=(rsB.s(mmi), 16))
            hbo_sl.set(mmi, rsB.s(mmi))

        def emit_plain_store(m, dest):
            p.w("y", ac)
            p.o("y", "dma_start", dest[128 * m:128 * (m + 1), :], bufs["hbr"][:],
                inc=(dP, 16))
            hbr_sl.set(0, dP)

        def emit_tm_store(m, dest, mmi):
            hbr, vtt, psBT = bufs["hbr"], bufs["vtt"], bufs["psBT"]
            for n in range(N4):
                tpar = (mmi * N4 + n) % 2
                p.w("t", ac)
                psBT_sl.wait(p, "t", mmi * N4 + n)
                for j in range(4):
                    p.o("t", "transpose", psBT[tpar][:, 128 * j:128 * (j + 1)],
                        hbr[:, 512 * n + 128 * j:512 * n + 128 * (j + 1)],
                        eyeb[:], inc=(pe, 1) if j == 3 else None)
                p.w("v", pe)
                vtt_sl.wait(p, "v", mmi * N4 + n)
                vtv = vtt[:, tpar * 512:(tpar + 1) * 512]
                p.o("v", "tensor_copy", vtv, psBT[tpar][:], inc=(ve, 1))
                psBT_sl.set(mmi * N4 + n, ve)
                p.w("y", ve)
                dseg = dest[512 * n:512 * (n + 1), 128 * m:128 * (m + 1)]
                p.o("y", "dma_start", dseg.rearrange("(j p) f -> p j f", p=128),
                    vtv.rearrange("p (j f) -> p j f", j=4),
                    inc=(rsV.s(mmi * N4 + n), 16))
                vtt_sl.set(mmi * N4 + n, rsV.s(mmi * N4 + n))
            hbr_sl.set(0, pe)

        def ln1_rhs(k, n):
            return LN1T[:, T * k + 512 * n:T * k + 512 * (n + 1)]

        mmi = 0
        for (wsrc, MCH, kind, dest) in [
                (wq_d, NH, "rope_q", QT),
                (wk_d, NKV, "rope_k", KT),
                (wlq_d, NQC, "plain", LQT),
                (wv_d, NKV, "tm", VT),
                (wlv_d, LH, "tm", LVT)]:
            for m in range(MCH):
                emit_fm_chunk(wsrc, m, DC, mmi, ln1_rhs)
                if kind == "rope_q":
                    emit_rope_store(m, 0, dest, mmi)
                elif kind == "rope_k":
                    emit_rope_store(m, 2, dest, mmi)
                elif kind == "plain":
                    emit_plain_store(m, dest)
                else:
                    emit_tm_store(m, dest, mmi)
                mmi += 1

        barrier()
        if "ln1t_dbg" in dbg:
            ln1t_dbg = dram("ln1t_dbg", [D, T], BF16, kind="ExternalOutput")
            for c in range(DC):
                p.o("y", "dma_start", ln1t_dbg[128 * c:128 * (c + 1), :],
                    LN1T[:, T * c:T * (c + 1)], inc=(dS, 16))
            barrier()
        stB.close()
        stAB.close()
        stB = ExitStack()
        alloc_b(stB, 0)
        wfm_sl = Slot(2)
        psB_sl = Slot(2)
        hbr_sl = Slot(1)
        hbs_sl = Slot(1)
        hbo_sl = Slot(2)
        lqs = stB.enter_context(nc.sbuf_tensor("lqs", [128, NQC * T], BF16))
        for k in range(NQC):
            p.o("y", "dma_start", lqs[:, T * k:T * (k + 1)],
                LQT[128 * k:128 * (k + 1), :], inc=(dX, 16))
        p.w("t", dX)

        def lq_rhs(k, n):
            return lqs[:, T * k + 512 * n:T * k + 512 * (n + 1)]

        for m in range(LH):
            emit_fm_chunk(wlk_d, m, NQC, mmi, lq_rhs)
            emit_rope_store(m, 2, LKT, mmi)
            mmi += 1

        barrier()
        stB.close()
    else:
        stAB.close()

    if "C" in phases:
        stC = ExitStack()
        qh = stC.enter_context(nc.sbuf_tensor("qh", [128, 4 * T], BF16))
        kh = stC.enter_context(nc.sbuf_tensor("kh", [128, 2 * T], BF16))
        lkh = stC.enter_context(nc.sbuf_tensor("lkh", [128, 4 * T], BF16))
        vh = stC.enter_context(nc.sbuf_tensor("vh", [128, 2 * T], BF16))
        lvh = stC.enter_context(nc.sbuf_tensor("lvh", [128, 4 * T], BF16))
        PT = stC.enter_context(nc.sbuf_tensor("PT", [128, 2 * 512], BF16))
        ohb = stC.enter_context(nc.sbuf_tensor("ohb", [128, 4 * T], BF16))
        sumall = stC.enter_context(nc.sbuf_tensor("sumall", [1, 4 * T], F32))
        recb = stC.enter_context(nc.sbuf_tensor("recb", [1, T], BF16))
        SA = [stC.enter_context(nc.psum_tensor(f"SA{i}", [128, 512], F32))
              for i in range(2)]
        OA = [stC.enter_context(nc.psum_tensor(f"OA{i}", [128, 512], F32))
              for i in range(2)]
        SU = [stC.enter_context(nc.psum_tensor(f"SU{i}", [1, 512], F32))
              for i in range(2)]
        RB = stC.enter_context(nc.psum_tensor("RB", [128, 512], F32))
        SA_sl = Slot(2)
        PT_sl = Slot(2)
        OA_sl = Slot(2)
        SU_sl = Slot(2)
        RB_sl = Slot(1)
        oh_sl = Slot(4)
        pair_done = {}

        def attn_quad(qvs, kbs, vbs, rows):
            sstep = 0
            ce_hist = {}
            for m in range(N4):
                for u in range(4):
                    hh = u // 2
                    qv = qvs[hh]
                    ktiles = 4 * m + 4
                    useq = 4 * m + u
                    oa = OA[u % 2]
                    su = SU[u % 2]
                    OA_sl.wait(p, "t", useq)
                    SU_sl.wait(p, "t", useq)
                    for t in range(ktiles + 1):
                        if t < ktiles:
                            sa = SA[sstep % 2]
                            SA_sl.wait(p, "t", sstep)
                            o = t - 4 * m
                            if o < 0:
                                p.o("t", "matmul", sa[:],
                                    kbs[u][:, 128 * t:128 * (t + 1)],
                                    qv[:, 512 * m:512 * (m + 1)],
                                    start=True, stop=True, inc=(cS, 1))
                            else:
                                p.o("t", "matmul", sa[:],
                                    kbs[u][:, 128 * t:128 * (t + 1)],
                                    qv[:, 512 * m:512 * (m + 1)],
                                    start=True, stop=False)
                                for j in range(o + 1):
                                    lt = negalb if j < o else maskub
                                    p.o("t", "matmul",
                                        sa[:, 128 * j:128 * (j + 1)],
                                        lt[:], eyeb[:], start=False,
                                        stop=(j == o),
                                        inc=(cS, 1) if j == o else None)
                            p.w("s", cS)
                            PT_sl.wait(p, "s", sstep)
                            ptv = PT[:, (sstep % 2) * 512:(sstep % 2 + 1) * 512]
                            p.o("s", "activation", ptv, sa[:], AF.Exp,
                                inc=(cE, 1))
                            ce_hist[sstep] = cE.t
                            SA_sl.set(sstep, cE)
                        if t >= 1:
                            ts = sstep - 1 if t == ktiles else sstep - 1
                            tp_ = t - 1
                            pstep = sstep - 1 if t < ktiles else sstep - 1
                            p.w("t", cE, ce_hist[pstep])
                            pv_ = PT[:, (pstep % 2) * 512:(pstep % 2 + 1) * 512]
                            p.o("t", "matmul", oa[:],
                                vbs[u][:, 128 * tp_:128 * (tp_ + 1)], pv_,
                                start=(tp_ == 0), stop=(tp_ == ktiles - 1),
                                inc=(cPV, 1) if tp_ == ktiles - 1 else None)
                            p.o("t", "matmul", su[0:1, :], onesb[:, 0:1], pv_,
                                start=(tp_ == 0), stop=(tp_ == ktiles - 1),
                                inc=(pe, 1))
                            PT_sl.set(pstep, pe)
                        if t < ktiles:
                            sstep += 1
                    p.w("v", cPV)
                    if m == 0:
                        oh_sl.wait(p, "v", u)
                    p.o("v", "tensor_copy",
                        ohb[:, u * T + 512 * m:u * T + 512 * (m + 1)],
                        oa[:], inc=(cOe, 1))
                    OA_sl.set(useq, cOe)
                    p.w("v", pe)
                    p.o("v", "tensor_copy",
                        sumall[0:1, u * T + 512 * m:u * T + 512 * (m + 1)],
                        su[0:1, :], inc=(cTeV, 1))
                    SU_sl.set(useq, cTeV)
            for u in range(4):
                p.o("v", "reciprocal", sumall[0:1, u * T:(u + 1) * T],
                    sumall[0:1, u * T:(u + 1) * T], cwait=True, cinc=True)
                p.w("s", ve)
                p.o("s", "copy", recb[0:1, :], sumall[0:1, u * T:(u + 1) * T],
                    inc=(ac, 1))
                for c in range(N4):
                    RB_sl.wait(p, "t", 0)
                    p.w("t", ac)
                    p.o("t", "matmul", RB[:], onesb[0:1, 0:128],
                        recb[0:1, 512 * c:512 * (c + 1)],
                        start=True, stop=True, inc=(cT, 1))
                    p.w("v", cT)
                    p.w("v", cOe)
                    p.o("v", "tensor_tensor",
                        ohb[:, u * T + 512 * c:u * T + 512 * (c + 1)],
                        ohb[:, u * T + 512 * c:u * T + 512 * (c + 1)],
                        RB[:], ALU.mult, inc=(ve, 1), cwait=True)
                    RB_sl.set(0, ve)
                p.w("y", ve)
                p.o("y", "dma_start",
                    AOT[128 * rows[u]:128 * (rows[u] + 1), :],
                    ohb[:, u * T:(u + 1) * T], inc=(rsO.s(u), 16))
                oh_sl.set(u, rsO.s(u))

        for h0 in range(0, NH, 2):
            kvi = h0 // 4
            pp = (h0 // 2) % 2
            guards = pair_done.get(h0 - 4, [])
            p.wg("y", guards)
            for hh in (0, 1):
                h = h0 + hh
                b = 2 * pp + hh
                p.o("y", "dma_start", qh[:, b * T:(b + 1) * T],
                    QT[128 * h:128 * (h + 1), :], inc=(rq.s(b), 16))
                p.o("y", "dma_start", lkh[:, b * T:(b + 1) * T],
                    LKT[128 * h:128 * (h + 1), :], inc=(rl.s(b), 16))
                lvseg = LVT[:, 128 * h:128 * (h + 1)]
                lvv = lvh[:, b * T:(b + 1) * T]
                p.o("y", "dma_start", lvv.rearrange("p (t c) -> p t c", t=NT),
                    lvseg.rearrange("(t p) c -> p t c", p=128),
                    inc=(rl.s(b), 16))
            do_k0 = (h0 % 4 == 0)
            if do_k0:
                p.wg("y", pair_done.get(h0 - 8, []))
                p.o("y", "dma_start", kh[:, (kvi % 2) * T:(kvi % 2 + 1) * T],
                    KT[128 * kvi:128 * (kvi + 1), :], inc=(rk.s(kvi), 16))
                vseg = VT[:, 128 * kvi:128 * (kvi + 1)]
                vhv = vh[:, (kvi % 2) * T:(kvi % 2 + 1) * T]
                p.o("y", "dma_start", vhv.rearrange("p (t c) -> p t c", t=NT),
                    vseg.rearrange("(t p) c -> p t c", p=128),
                    inc=(rk.s(kvi), 16))
            for e in ("t", "s"):
                p.w(e, rq.s(2 * pp)); p.w(e, rq.s(2 * pp + 1))
                p.w(e, rk.s(kvi))
                p.w(e, rl.s(2 * pp)); p.w(e, rl.s(2 * pp + 1))
            kvv = kh[:, (kvi % 2) * T:(kvi % 2 + 1) * T]
            vvv = vh[:, (kvi % 2) * T:(kvi % 2 + 1) * T]
            qvs = (qh[:, 2 * pp * T:(2 * pp + 1) * T],
                   qh[:, (2 * pp + 1) * T:(2 * pp + 2) * T])
            kbs = (kvv, lkh[:, 2 * pp * T:(2 * pp + 1) * T], kvv,
                   lkh[:, (2 * pp + 1) * T:(2 * pp + 2) * T])
            vbs = (vvv, lvh[:, 2 * pp * T:(2 * pp + 1) * T], vvv,
                   lvh[:, (2 * pp + 1) * T:(2 * pp + 2) * T])
            rows = (h0, NH + h0, h0 + 1, NH + h0 + 1)
            attn_quad(qvs, kbs, vbs, rows)
            pair_done[h0] = [(cS, cS.t), (cPV, cPV.t), (ac, ac.t),
                             (ve, ve.t), (pe, pe.t)]

        barrier()
        stC.close()

    if "D" in phases:
        stD1 = ExitStack()
        rhsD = stD1.enter_context(nc.sbuf_tensor("rhsD", [128, 32 * T], BF16))
        wodb = stD1.enter_context(nc.sbuf_tensor("wodb", [128, 2 * 32 * 128], BF16))
        aob = stD1.enter_context(nc.sbuf_tensor("aob", [128, 2 * 512], F32))
        psD1 = [stD1.enter_context(nc.psum_tensor(f"psD1{i}", [128, 512], F32))
                for i in range(2)]
        wod_sl = Slot(2)
        aob_sl = Slot(2)
        psD_sl = Slot(2)

        for kg in range(2):
            p.w("y", pe)
            for r in range(8):
                seg = AOT[4096 * kg + 512 * r:4096 * kg + 512 * (r + 1), :]
                p.o("y", "dma_start",
                    rhsD[:, 4 * r * T:4 * (r + 1) * T].rearrange(
                        "p (k t) -> p k t", k=4),
                    seg.rearrange("(k p) t -> p k t", p=128), inc=(dX, 16))
            for m in range(DC):
                par = m % 2
                wod_sl.wait(p, "g", m)
                wsl = wolo_d[4096 * kg:4096 * (kg + 1), 128 * m:128 * (m + 1)]
                wdst = wodb[:, par * 32 * 128:(par + 1) * 32 * 128]
                p.o("g", "dma_start", wdst.rearrange("p (k m) -> p k m", k=32),
                    wsl.rearrange("(k p) m -> p k m", p=128),
                    inc=(rwD.s(m), 16))
                p.w("t", rwD.s(m))
                if m == 0:
                    p.w("t", dX)
                for n in range(N4):
                    psD_sl.wait(p, "t", n)
                    for k in range(32):
                        p.o("t", "matmul", psD1[n % 2][:],
                            wodb[:, par * 32 * 128 + 128 * k:
                                 par * 32 * 128 + 128 * (k + 1)],
                            rhsD[:, k * T + 512 * n:k * T + 512 * (n + 1)],
                            start=(k == 0), stop=(k == 31),
                            inc=(pe, 1) if k == 31 else None)
                    p.w("s", pe)
                    aob_sl.wait(p, "s", n)
                    av = aob[:, (n % 2) * 512:(n % 2 + 1) * 512]
                    p.o("s", "copy", av, psD1[n % 2][:], inc=(ac, 1))
                    psD_sl.set(n, ac)
                    p.w("y", ac)
                    p.o("y", "dma_start",
                        AOP[kg, 128 * m:128 * (m + 1), 512 * n:512 * (n + 1)],
                        av, inc=(rsA1.s(n), 16))
                    aob_sl.set(n, rsA1.s(n))
                wod_sl.set(m, pe)
        barrier()
        stD1.close()

    stDE = ExitStack()
    XFT = stDE.enter_context(nc.sbuf_tensor("XFT", [128, NQC * T], BF16))

    if "D" in phases:
        stD2 = ExitStack()
        pbuf = stD2.enter_context(nc.sbuf_tensor("pbuf", [128, 2 * D], F32))
        accD = stD2.enter_context(nc.sbuf_tensor("accD", [128, D], F32))
        atok = stD2.enter_context(nc.sbuf_tensor("atok", [128, D], F32))
        xrD = stD2.enter_context(nc.sbuf_tensor("xrD", [128, D], F32))
        ln2b = stD2.enter_context(nc.sbuf_tensor("ln2b", [128, D], BF16))
        xffold = stD2.enter_context(nc.sbuf_tensor("xffold", [128, LAT], F32))
        xfb = stD2.enter_context(nc.sbuf_tensor("xfb", [128, LAT], BF16))
        junkD = stD2.enter_context(nc.sbuf_tensor("junkD", [128, D], BF16))
        l2st = stD2.enter_context(nc.sbuf_tensor("l2st", [128, 2 * 1024], BF16))
        psDT = [stD2.enter_context(nc.psum_tensor(f"psDT{i}", [128, 512], F32))
                for i in range(2)]
        psDB = [stD2.enter_context(nc.psum_tensor(f"psDB{i}", [128, 1024], BF16))
                for i in range(2)]
        pb_sl = Slot(2)
        psDT_sl = Slot(2)
        psDB_sl = Slot(2)
        l2st_sl = Slot(2)
        xrD_sl = Slot(1)

        for ti in range(NT):
            for kg in range(2):
                pb_sl.wait(p, "y", 2 * ti + kg)
                pv = pbuf[:, (kg % 2) * D:(kg % 2 + 1) * D]
                seg = AOP[kg, :, 128 * ti:128 * (ti + 1)]
                p.o("y", "dma_start", pv.rearrange("p (c t) -> p c t", c=DC),
                    seg.rearrange("(c p) t -> p c t", p=128),
                    inc=(rpD.s(2 * ti + kg), 16))
            p.w("v", rpD.s(2 * ti))
            p.w("v", rpD.s(2 * ti + 1))
            p.w("v", ac)
            p.w("v", pe)
            p.o("v", "tensor_tensor", accD[:],
                pbuf[:, 0:D], pbuf[:, D:2 * D], ALU.add,
                inc=(ve, 1), cwait=True, cinc=True)
            pb_sl.set(2 * ti, ve)
            pb_sl.set(2 * ti + 1, ve)
            xrD_sl.wait(p, "y", ti)
            p.o("y", "dma_start", xrD[:], x_in[128 * ti:128 * (ti + 1), :],
                inc=(rxD.s(ti), 16))
            p.w("t", ve)
            for g in range(8):
                par = g % 2
                psDT_sl.wait(p, "t", g)
                for j in range(4):
                    c = 4 * g + j
                    p.o("t", "transpose", psDT[par][:, 128 * j:128 * (j + 1)],
                        accD[:, 128 * c:128 * (c + 1)], eyef[:],
                        inc=(pe, 1) if j == 3 else None)
                p.w("v", pe)
                p.o("v", "tensor_copy", atok[:, 512 * g:512 * (g + 1)],
                    psDT[par][:], inc=(ve, 1))
                psDT_sl.set(g, ve)
            p.w("s", ve)
            p.w("v", ve)
            ln_stats_apply(atok[:], atok[:], junkD[:])
            p.w("v", ac)
            p.w("v", rxD.s(ti))
            p.o("v", "tensor_tensor", xrD[:], xrD[:], atok[:], ALU.add,
                inc=(ve, 1), cwait=True, cinc=True)
            p.w("y", ve)
            p.o("y", "dma_start", X2[128 * ti:128 * (ti + 1), :], xrD[:],
                inc=(rsX2.s(ti), 16))
            ln_stats_apply(xrD[:], atok[:], junkD[:])
            xrD_sl.set(ti, ac, rsX2.s(ti))
            p.w("v", ac)
            p.o("v", "tensor_copy", ln2b[:], atok[:], cwait=True, cinc=True)
            p.w("v", ac)
            p.o("v", "tensor_tensor", xffold[:], atok[:, 0:LAT],
                atok[:, LAT:2 * LAT], ALU.add, cwait=True, cinc=True)
            p.o("v", "tensor_tensor", xffold[:], xffold[:],
                atok[:, 2 * LAT:3 * LAT], ALU.add, cwait=True, cinc=True)
            p.o("v", "tensor_tensor", xffold[:], xffold[:],
                atok[:, 3 * LAT:4 * LAT], ALU.add, cwait=True, cinc=True)
            p.o("v", "tensor_copy", xfb[:], xffold[:], inc=(ve, 1), cwait=True)
            p.w("t", ac)
            p.w("t", ve)
            for g in range(4):
                par = g % 2
                psDB_sl.wait(p, "t", g)
                for j in range(8):
                    c = 8 * g + j
                    p.o("t", "transpose", psDB[par][:, 128 * j:128 * (j + 1)],
                        ln2b[:, 128 * c:128 * (c + 1)], eyeb[:],
                        inc=(pe, 1) if j == 7 else None)
                p.w("v", pe)
                l2st_sl.wait(p, "v", 4 * ti + g)
                lsv = l2st[:, (g % 2) * 1024:(g % 2 + 1) * 1024]
                p.o("v", "tensor_copy", lsv, psDB[par][:], inc=(ve, 1))
                psDB_sl.set(g, ve)
                p.w("y", ve)
                seg = LN2T[128 * 8 * g:128 * 8 * (g + 1),
                           128 * ti:128 * (ti + 1)]
                p.o("y", "dma_start", seg.rearrange("(c p) t -> p c t", p=128),
                    lsv.rearrange("p (c t) -> p c t", c=8),
                    inc=(rsL.s(4 * ti + g), 16))
                l2st_sl.set(4 * ti + g, rsL.s(4 * ti + g))
            psDB_sl.wait(p, "t", 4)
            for j in range(8):
                p.o("t", "transpose", psDB[0][:, 128 * j:128 * (j + 1)],
                    xfb[:, 128 * j:128 * (j + 1)], eyeb[:],
                    inc=(pe, 1) if j == 7 else None)
            p.w("v", pe)
            segx = XFT.rearrange("p (c t) -> p c t", c=NQC)
            p.o("v", "tensor_copy", segx[:, :, 128 * ti:128 * (ti + 1)],
                psDB[0].rearrange("p (c t) -> p c t", c=8), inc=(ve, 1))
            psDB_sl.set(4, ve)

        barrier()
        stD2.close()

    if "E" in phases:
        stE = ExitStack()
        a13b = stE.enter_context(nc.sbuf_tensor("a13b", [128, DC * 4], BF16))
        b1b = stE.enter_context(nc.sbuf_tensor("b1b", [2, HF], BF16))
        b3b = stE.enter_context(nc.sbuf_tensor("b3b", [2, HF], BF16))
        lnsE = stE.enter_context(nc.sbuf_tensor("lnsE", [128, 2 * 512], BF16))
        xgf = stE.enter_context(nc.sbuf_tensor("xgf", [2, T], F32))
        xuf = stE.enter_context(nc.sbuf_tensor("xuf", [2, T], F32))
        xgb = stE.enter_context(nc.sbuf_tensor("xgb", [2, T], BF16))
        xub = stE.enter_context(nc.sbuf_tensor("xub", [2, T], BF16))
        w13 = stE.enter_context(nc.sbuf_tensor("w13", [128, 2 * 16 * 128], BF16))
        sgb = stE.enter_context(nc.sbuf_tensor("sgb", [128, 2 * 512], BF16))
        upb = stE.enter_context(nc.sbuf_tensor("upb", [128, 2 * 512], BF16))
        gur = stE.enter_context(nc.sbuf_tensor("gur", [128, 2 * T], BF16))
        psE = [stE.enter_context(nc.psum_tensor(f"psE{i}", [128, 512], F32))
               for i in range(4)]
        psEA1 = stE.enter_context(nc.psum_tensor("psEA1", [2, 512], F32))
        psEA2 = stE.enter_context(nc.psum_tensor("psEA2", [2, 512], F32))
        lnsE_sl = Slot(2)
        w13_sl = Slot(2)
        gur_sl = Slot(2)

        p.o("g", "dma_start", a13b[:].rearrange("p (c r) -> p c r", c=DC),
            a13t_d.rearrange("(c p) r -> p c r", p=128), inc=(dC2, 16))
        p.o("g", "dma_start", b1b[:], b1t_d, inc=(dC2, 16))
        p.o("g", "dma_start", b3b[:], b3t_d, inc=(dC2, 16))

        p.w("t", dC2)
        for n in range(N4):
            p.w("t", ve)
            for k in range(DC):
                ci = n * DC + k
                lnsE_sl.wait(p, "y", ci)
                lv = lnsE[:, (ci % 2) * 512:(ci % 2 + 1) * 512]
                p.o("y", "dma_start", lv,
                    LN2T[128 * k:128 * (k + 1), 512 * n:512 * (n + 1)],
                    inc=(rlE.s(ci), 16))
                p.w("t", rlE.s(ci))
                p.o("t", "matmul", psEA1[0:2, :], a13b[:, 4 * k:4 * k + 2],
                    lv, start=(k == 0), stop=(k == DC - 1),
                    inc=None)
                p.o("t", "matmul", psEA2[0:2, :], a13b[:, 4 * k + 2:4 * k + 4],
                    lv, start=(k == 0), stop=(k == DC - 1),
                    inc=(pe, 1))
                lnsE_sl.set(ci, pe)
            p.w("v", pe)
            p.o("v", "tensor_copy", xgf[:, 512 * n:512 * (n + 1)],
                psEA1[0:2, :])
            p.o("v", "tensor_copy", xuf[:, 512 * n:512 * (n + 1)],
                psEA2[0:2, :], inc=(ve, 1))
        p.w("s", ve)
        p.o("s", "mul", xgb[:], xgf[:], 0.5)
        p.o("s", "mul", xub[:], xuf[:], 0.5, inc=(ac, 1))
        xa13_guard = [(ac, ac.t)]
        psE_sl = Slot(2)
        sg_sl = Slot(2)

        for m in range(HC):
            par = m % 2
            w13_sl.wait(p, "g", m)
            wdst1 = w13[:, par * 16 * 128:par * 16 * 128 + NQC * 128]
            wdst3 = w13[:, par * 16 * 128 + NQC * 128:(par + 1) * 16 * 128]
            p.o("g", "dma_start", wdst1.rearrange("p (k m) -> p k m", k=NQC),
                w1t_d[:, 128 * m:128 * (m + 1)].rearrange("(k p) m -> p k m", p=128),
                inc=(rwE.s(m), 16))
            p.o("g", "dma_start", wdst3.rearrange("p (k m) -> p k m", k=NQC),
                w3t_d[:, 128 * m:128 * (m + 1)].rearrange("(k p) m -> p k m", p=128),
                inc=(rwE.s(m), 16))
            p.w("t", rwE.s(m))
            p.wg("t", xa13_guard)
            for n in range(N4):
                gp = psE[2 * (n % 2)]
                up = psE[2 * (n % 2) + 1]
                psE_sl.wait(p, "t", n)
                for k in range(NQC):
                    p.o("t", "matmul", gp[:],
                        w13[:, par * 16 * 128 + 128 * k:
                            par * 16 * 128 + 128 * (k + 1)],
                        XFT[:, T * k + 512 * n:T * k + 512 * (n + 1)],
                        start=(k == 0), stop=False)
                p.o("t", "matmul", gp[:], b1b[0:2, 128 * m:128 * (m + 1)],
                    xgb[0:2, 512 * n:512 * (n + 1)], start=False, stop=True,
                    inc=(pe, 1))
                for k in range(NQC):
                    p.o("t", "matmul", up[:],
                        w13[:, par * 16 * 128 + NQC * 128 + 128 * k:
                            par * 16 * 128 + NQC * 128 + 128 * (k + 1)],
                        XFT[:, T * k + 512 * n:T * k + 512 * (n + 1)],
                        start=(k == 0), stop=False)
                p.o("t", "matmul", up[:], b3b[0:2, 128 * m:128 * (m + 1)],
                    xub[0:2, 512 * n:512 * (n + 1)], start=False, stop=True,
                    inc=(pe, 1))
                p.w("s", pe)
                sg_sl.wait(p, "s", n)
                sgv = sgb[:, (n % 2) * 512:(n % 2 + 1) * 512]
                upv = upb[:, (n % 2) * 512:(n % 2 + 1) * 512]
                p.o("s", "activation", sgv, gp[:], AF.Silu)
                p.o("s", "copy", upv, up[:], inc=(ac, 1))
                psE_sl.set(n, ac)
                p.w("v", ac)
                gur_sl.wait(p, "v", m)
                gv = gur[:, (m % 2) * T + 512 * n:(m % 2) * T + 512 * (n + 1)]
                p.o("v", "tensor_tensor", gv, sgv, upv, ALU.mult,
                    inc=(ve, 1))
                sg_sl.set(n, ve)
            w13_sl.set(m, pe)
            p.w("y", ve)
            p.o("y", "dma_start", GU[128 * m:128 * (m + 1), :],
                gur[:, (m % 2) * T:(m % 2 + 1) * T], inc=(rsG.s(m), 16))
            gur_sl.set(m, rsG.s(m))

        barrier()
        stE.close()
    stDE.close()

    if "F" in phases:
        stF = ExitStack()
        yfa = stF.enter_context(nc.sbuf_tensor("yfa", [128, YC * 512], F32))
        yfb = stF.enter_context(nc.sbuf_tensor("yfb", [128, YC * 512], BF16))
        gub = stF.enter_context(nc.sbuf_tensor("gub", [128, 3 * 512], BF16))
        wd2 = stF.enter_context(nc.sbuf_tensor("wd2", [128, 2 * YC * 128], BF16))
        a2sb = stF.enter_context(nc.sbuf_tensor("a2sb", [128, 2 * 88], BF16))
        b2sb = stF.enter_context(nc.sbuf_tensor("b2sb", [2, D], BF16))
        xa2f = stF.enter_context(nc.sbuf_tensor("xa2f", [2, 512], F32))
        xa2b = stF.enter_context(nc.sbuf_tensor("xa2b", [2, 512], BF16))
        fof = stF.enter_context(nc.sbuf_tensor("fof", [128, 2 * 512], F32))
        ftok = stF.enter_context(nc.sbuf_tensor("ftok", [128, 4 * D], F32))
        xrF = stF.enter_context(nc.sbuf_tensor("xrF", [128, 2 * D], F32))
        junkF = stF.enter_context(nc.sbuf_tensor("junkF", [128, D], BF16))
        psF = [stF.enter_context(nc.psum_tensor(f"psF{i}", [128, 512], F32))
               for i in range(2)]
        psFT = [stF.enter_context(nc.psum_tensor(f"psFT{i}", [128, 512], F32))
                for i in range(2)]
        psFA = stF.enter_context(nc.psum_tensor("psFA", [2, 512], F32))
        gub_sl = Slot(3)
        wd2_sl = Slot(2)
        psF_sl = Slot(2)
        fof_sl = Slot(2)
        psFT_sl = Slot(2)
        xr_sl = Slot(2)

        p.o("g", "dma_start", a2sb[:], a2r_d, inc=(dC2, 16))
        p.o("g", "dma_start", b2sb[:], b2t_d, inc=(dC2, 16))
        p.w("t", dC2)

        gu_tiles = []
        for j in range(YC):
            R = 128 if j < YC - 1 else 64
            for r in range(4):
                gu_tiles.append((j, r, YF * r + 128 * j, R))

        NTl = len(gu_tiles)
        for n in range(N4):
            p.w("t", ve)
            for c, (j, r, off, R) in enumerate(gu_tiles):
                ci = n * NTl + c
                gub_sl.wait(p, "y", ci)
                gv = gub[0:R, (c % 3) * 512:(c % 3 + 1) * 512]
                p.o("y", "dma_start", gv,
                    GU[off:off + R, 512 * n:512 * (n + 1)],
                    inc=(rgF.s(ci), 16))
                p.w("t", rgF.s(ci))
                p.o("t", "matmul", psFA[0:2, :], a2sb[0:R, 2 * c:2 * (c + 1)],
                    gv, start=(c == 0), stop=(c == NTl - 1),
                    inc=(pe, 1))
                p.w("v", rgF.s(ci))
                dst = yfa[0:R, 512 * j:512 * (j + 1)]
                if r == 0:
                    p.o("v", "tensor_copy", dst, gv, inc=(ve, 1), cwait=True,
                        cinc=True)
                else:
                    p.o("v", "tensor_tensor", dst, dst, gv, ALU.add,
                        inc=(ve, 1), cwait=True, cinc=True)
                gub_sl.set(ci, ve, pe)
            p.w("v", ve)
            p.o("v", "tensor_copy", yfb[:, 0:(YC - 1) * 512],
                yfa[:, 0:(YC - 1) * 512], cwait=True, cinc=True)
            p.o("v", "tensor_copy", yfb[0:64, (YC - 1) * 512:YC * 512],
                yfa[0:64, (YC - 1) * 512:YC * 512], inc=(ve, 1), cwait=True)
            p.w("v", pe)
            p.o("v", "tensor_copy", xa2f[:], psFA[0:2, :], inc=(ve, 1))
            p.w("s", ve)
            p.o("s", "mul", xa2b[:], xa2f[:], 0.5, inc=(ac, 1))
            blk_guard = [(ac, ac.t), (ve, ve.t)]
            for m in range(DC):
                par = m % 2
                wd2_sl.wait(p, "g", m)
                wv_ = wd2[:, par * YC * 128:(par + 1) * YC * 128]
                p.o("g", "dma_start",
                    wv_[:, 0:21 * 128].rearrange("p (k m) -> p k m", k=21),
                    w2t_d[0:2688, 128 * m:128 * (m + 1)].rearrange(
                        "(k p) m -> p k m", p=128), inc=(rwF.s(m), 16))
                p.o("g", "dma_start",
                    wv_[0:64, 21 * 128:22 * 128],
                    w2t_d[2688:2752, 128 * m:128 * (m + 1)], inc=(rwF.s(m), 16))
                p.w("t", rwF.s(m))
                p.wg("t", blk_guard)
                pp = psF[m % 2]
                psF_sl.wait(p, "t", m)
                for k in range(21):
                    p.o("t", "matmul", pp[:],
                        wv_[:, 128 * k:128 * (k + 1)],
                        yfb[:, 512 * k:512 * (k + 1)],
                        start=(k == 0), stop=False)
                p.o("t", "matmul", pp[:], wv_[0:64, 21 * 128:22 * 128],
                    yfb[0:64, 21 * 512:22 * 512], start=False, stop=False)
                p.o("t", "matmul", pp[:], b2sb[0:2, 128 * m:128 * (m + 1)],
                    xa2b[0:2, :], start=False, stop=True, inc=(pe, 1))
                wd2_sl.set(m, pe)
                p.w("s", pe)
                fof_sl.wait(p, "s", m)
                fv = fof[:, (m % 2) * 512:(m % 2 + 1) * 512]
                p.o("s", "copy", fv, pp[:], inc=(ac, 1))
                psF_sl.set(m, ac)
                p.w("t", ac)
                psFT_sl.wait(p, "t", m)
                for j in range(4):
                    p.o("t", "transpose", psFT[m % 2][:, 128 * j:128 * (j + 1)],
                        fv[:, 128 * j:128 * (j + 1)], eyef[:],
                        inc=(pe, 1) if j == 3 else None)
                fof_sl.set(m, pe)
                p.w("v", pe)
                p.o("v", "tensor_copy",
                    ftok.rearrange("p (j f) -> p j f", j=4)[:, :, 128 * m:128 * (m + 1)],
                    psFT[m % 2].rearrange("p (j f) -> p j f", j=4), inc=(ve, 1))
                psFT_sl.set(m, ve)
            for j in range(4):
                ti = 4 * n + j
                xr_sl.wait(p, "y", ti)
                xv = xrF[:, (j % 2) * D:(j % 2 + 1) * D]
                p.o("y", "dma_start", xv, X2[128 * ti:128 * (ti + 1), :],
                    inc=(rxF.s(ti), 16))
                ft = ftok[:, j * D:(j + 1) * D]
                p.w("v", ve)
                p.w("s", ve)
                ln_stats_apply(ft, ft, junkF[:])
                p.w("v", ac)
                p.w("v", rxF.s(ti))
                p.o("v", "tensor_tensor", xv, xv, ft, ALU.add, inc=(ve, 1),
                    cwait=True, cinc=True)
                p.w("y", ve)
                p.o("y", "dma_start", y_out[128 * ti:128 * (ti + 1), :], xv,
                    inc=(rsY.s(ti), 16))
                xr_sl.set(ti, rsY.s(ti))
            p.w("v", ac)

        stF.close()

    for s in sems.values():
        p.w("y", s)

    with nc.Block() as block:
        @block.tensor
        def _(h):
            p.replay("t", h)

        @block.vector
        def _(h):
            p.replay("v", h)

        @block.scalar
        def _(h):
            p.replay("s", h)

        @block.gpsimd
        def _(h):
            p.replay("g", h)

        @block.sync
        def _(h):
            p.replay("y", h)

    return nc, st



def _host_prep(inputs):
    f32 = np.float32
    c = np.ascontiguousarray
    hs = np.asarray(inputs["L0_attn_hidden_states"], dtype=f32)
    common = {
        "wq": c(np.asarray(inputs["L0_attn_w_q"], f32)),
        "wk": c(np.asarray(inputs["L0_attn_w_k"], f32)),
        "wv": c(np.asarray(inputs["L0_attn_w_v"], f32)),
        "wlq": c(np.asarray(inputs["L0_attn_w_lq"], f32)),
        "wlk": c(np.asarray(inputs["L0_attn_w_lk"], f32)),
        "wlv": c(np.asarray(inputs["L0_attn_w_lv"], f32)),
        "wolo": c(np.concatenate([np.asarray(inputs["L0_attn_w_o"], f32),
                                  np.asarray(inputs["L0_attn_w_lo"], f32)], 0)),
        "w1t": c(np.asarray(inputs["L0_ffn_w1q"], f32).T),
        "w3t": c(np.asarray(inputs["L0_ffn_w3q"], f32).T),
        "w2t": c(np.asarray(inputs["L0_ffn_w2q"], f32).T),
        "a13t": c(np.concatenate([np.asarray(inputs["L0_ffn_a1"], f32),
                                  np.asarray(inputs["L0_ffn_a3"], f32)], 0).T),
        "b1t": c(np.asarray(inputs["L0_ffn_b1"], f32).T),
        "b3t": c(np.asarray(inputs["L0_ffn_b3"], f32).T),
        "a2r": _a2_retile(np.asarray(inputs["L0_ffn_a2"], f32)),
        "b2t": c(np.asarray(inputs["L0_ffn_b2"], f32).T),
        "c_eyeb": np.eye(128).astype(ml_dtypes.bfloat16),
        "c_ones": np.ones((128, 129), np.float32).astype(ml_dtypes.bfloat16),
        "c_eyef": np.eye(128).astype(f32),
        "c_mask": np.triu(np.full((128, 128), NEGINF, f32), 1),
        "c_masku": np.triu(np.full((128, 128), NEGINF, f32), 1)
        .astype(ml_dtypes.bfloat16),
        "c_negall": np.full((128, 128), NEGINF, f32).astype(ml_dtypes.bfloat16),
        "c_rope": _rope_tables(),
    }
    return hs, common


def _a2_retile(a2):
    a2t = a2.T
    out = np.zeros((128, 2 * 88), np.float32)
    t = 0
    for j in range(YC):
        R = 128 if j < YC - 1 else 64
        for r in range(4):
            off = YF * r + 128 * j
            out[0:R, 2 * t:2 * t + 2] = a2t[off:off + R]
            t += 1
    return out


def _rope_tables():
    inv = 1.0 / (10000.0 ** (np.arange(0, HD, 2, dtype=np.float64) / HD))
    t = np.arange(T, dtype=np.float64)
    fr = np.outer(t, inv)
    emb = np.concatenate([fr, fr], axis=1)
    cos = np.cos(emb).T.astype(np.float32)
    sin = np.sin(emb).T.astype(np.float32)
    ss = sin.copy()
    ss[0:64] *= -1.0
    tab = np.empty((128, 4 * T), np.float32)
    tab[:, 0 * T:1 * T] = cos * SCALE
    tab[:, 1 * T:2 * T] = ss * SCALE
    tab[:, 2 * T:3 * T] = cos
    tab[:, 3 * T:4 * T] = ss
    return tab.astype(ml_dtypes.bfloat16)


_CACHE = {}


def kernel(**inputs):
    hs, common = _host_prep(inputs)
    nc, st = build_core()
    in_maps = []
    for b in range(8):
        m = dict(common)
        m["x"] = np.ascontiguousarray(hs[b])
        in_maps.append(m)
    res = run_bass_kernel_spmd(nc, in_maps, core_ids=list(range(8)))
    out = np.stack([np.asarray(r["y"]) for r in res.results], axis=0)
    return out.astype(np.float32)


# version 14
# speedup vs baseline: 1.0145x; 1.0145x over previous
import math
from contextlib import ExitStack

import numpy as np
import ml_dtypes

import concourse.bass as bass
import concourse.mybir as mybir
from concourse.bass_utils import run_bass_kernel_spmd

F32 = mybir.dt.float32
BF16 = mybir.dt.bfloat16
AF = mybir.ActivationFunctionType
ALU = mybir.AluOpType
AX = mybir.AxisListType

T = 2048
D = 4096
HD = 128
NH = 32
NKV = 8
LH = 32
LAT = 1024
HF = 11008
HC = HF // 128
NQC = LAT // 128
YF = HF // 4
YC = 22
DC = D // 128
NT = T // 128
N4 = T // 512
SCALE = 1.0 / math.sqrt(HD)
EPS = 1e-5
NEGINF = -1.0e30

ENGS = ("t", "v", "s", "g", "y")
DEBUG_SLOW = False


class Sem:
    def __init__(self, nc, st, name):
        self.h = st.enter_context(nc.semaphore(name))
        self.t = 0


class Prog:
    def __init__(self):
        self.q = {e: [] for e in ENGS}
        self.chain = {}

    def w(self, e, sem, val=None):
        if val is None:
            val = sem.t
        if val > 0:
            self.q[e].append(("w", sem, val))

    def wg(self, e, guards):
        for (sem, val) in guards:
            self.w(e, sem, val)

    def o(self, e, method, *args, inc=None, cinc=False, cwait=False, **kw):
        if cwait and e in self.chain:
            self.w(e, self.chain[e])
        incs = []
        if inc is not None:
            if isinstance(inc, tuple):
                inc = [inc]
            for (sem, k) in inc:
                sem.t += k
                incs.append((sem, k))
        if cinc and e in self.chain:
            cs = self.chain[e]
            if not any(s is cs for (s, _) in incs):
                assert not incs, "one sem update per instruction"
                cs.t += 1
                incs.append((cs, 1))
        self.q[e].append(("o", method, args, kw, incs))

    def replay(self, e, h):
        for it in self.q[e]:
            if it[0] == "w":
                h.wait_ge(it[1].h, it[2])
            else:
                _, m, args, kw, incs = it
                inst = getattr(h, m)(*args, **kw)
                for (sem, k) in incs:
                    inst = inst.then_inc(sem.h, k)


class Slot:

    def __init__(self, n):
        self.g = [[] for _ in range(n)]
        self.n = n

    def wait(self, p, e, i):
        p.wg(e, self.g[i % self.n])

    def set(self, i, *sems):
        self.g[i % self.n] = [(s, s.t) for s in sems]


def cdiv(a, b):
    return (a + b - 1) // b


class Ring:

    def __init__(self, S, name, n):
        self.sems = [S(name + str(i)) for i in range(n)]
        self.n = n

    def s(self, i):
        return self.sems[i % self.n]


def build_core(dbg=frozenset(), phases="ABCDEF"):
    nc = bass.Bass("TRN2", target_bir_lowering=False)
    st = ExitStack()
    p = Prog()

    def dram(name, shape, dtype, kind=None):
        if kind is None and name in dbg:
            kind = "ExternalOutput"
        if kind is None:
            h = nc.dram_tensor(name, shape, dtype)
        else:
            h = nc.dram_tensor(name, shape, dtype, kind=kind)
        return h.ap()

    x_in = dram("x", [T, D], F32, kind="ExternalInput")
    wq_d = dram("wq", [D, D], F32, kind="ExternalInput")
    wk_d = dram("wk", [D, LAT], F32, kind="ExternalInput")
    wv_d = dram("wv", [D, LAT], F32, kind="ExternalInput")
    wlq_d = dram("wlq", [D, LAT], F32, kind="ExternalInput")
    wlk_d = dram("wlk", [LAT, D], F32, kind="ExternalInput")
    wlv_d = dram("wlv", [D, D], F32, kind="ExternalInput")
    wolo_d = dram("wolo", [2 * D, D], F32, kind="ExternalInput")
    w1t_d = dram("w1t", [LAT, HF], F32, kind="ExternalInput")
    w3t_d = dram("w3t", [LAT, HF], F32, kind="ExternalInput")
    w2t_d = dram("w2t", [YF, D], F32, kind="ExternalInput")
    a13t_d = dram("a13t", [D, 4], F32, kind="ExternalInput")
    b1t_d = dram("b1t", [2, HF], F32, kind="ExternalInput")
    b3t_d = dram("b3t", [2, HF], F32, kind="ExternalInput")
    a2r_d = dram("a2r", [128, 2 * 88], F32, kind="ExternalInput")
    b2t_d = dram("b2t", [2, D], F32, kind="ExternalInput")
    c_eyeb = dram("c_eyeb", [128, 128], BF16, kind="ExternalInput")
    c_ones = dram("c_ones", [128, 129], BF16, kind="ExternalInput")
    c_eyef = dram("c_eyef", [128, 128], F32, kind="ExternalInput")
    c_mask = dram("c_mask", [128, 128], F32, kind="ExternalInput")
    c_masku = dram("c_masku", [128, 128], BF16, kind="ExternalInput")
    c_negall = dram("c_negall", [128, 128], BF16, kind="ExternalInput")
    c_rope = dram("c_rope", [128, 4 * T], BF16, kind="ExternalInput")

    QT = dram("QT", [NH * 128, T], BF16)
    KT = dram("KT", [NKV * 128, T], BF16)
    LKT = dram("LKT", [LH * 128, T], BF16)
    LQT = dram("LQT", [LAT, T], BF16)
    VT = dram("VT", [T, NKV * 128], BF16)
    LVT = dram("LVT", [T, LH * 128], BF16)
    AOT = dram("AOT", [2 * D, T], BF16)
    AOP = dram("AOP", [2, D, T], F32)
    X2 = dram("X2", [T, D], F32)
    LN2T = dram("LN2T", [D, T], BF16)
    GU = dram("GU", [HF, T], BF16)
    y_out = dram("y", [T, D], F32, kind="ExternalOutput")

    sems = {}

    def S(name):
        s = Sem(nc, st, name)
        sems[name] = s
        return s

    dW = S("dW"); dX = S("dX"); dH = S("dH"); dS = S("dS"); dR = S("dR")
    dO = S("dO"); po = S("po"); pe = S("pe"); ve = S("ve"); ac = S("ac")
    cS = S("cS"); cM = S("cM"); cE = S("cE"); cN = S("cN")
    cT = S("cT"); cTeV = S("cTeV"); cTeS = S("cTeS")
    cPV = S("cPV"); cOe = S("cOe")
    p.chain = {"v": ve, "s": ac, "g": po}
    dC = S("dC")
    dC2 = S("dC2")
    dP = S("dP")
    rxA = Ring(S, "rxA", 2)
    rwB = Ring(S, "rwB", 2)
    rsB = Ring(S, "rsB", 2)
    rsV = Ring(S, "rsV", 2)
    rq = Ring(S, "rq", 4)
    rk = Ring(S, "rk", 2)
    rl = Ring(S, "rl", 4)
    rsO = Ring(S, "rsO", 4)
    rwD = Ring(S, "rwD", 2)
    rsA1 = Ring(S, "rsA1", 2)
    rpD = Ring(S, "rpD", 2)
    rxD = Ring(S, "rxD", 2)
    rsX2 = Ring(S, "rsX2", 2)
    rsL = Ring(S, "rsL", 2)
    rlE = Ring(S, "rlE", 2)
    rwE = Ring(S, "rwE", 2)
    rsG = Ring(S, "rsG", 2)
    rgF = Ring(S, "rgF", 3)
    rwF = Ring(S, "rwF", 2)
    rxF = Ring(S, "rxF", 2)
    rsY = Ring(S, "rsY", 2)

    def barrier():
        for e in ENGS:
            for s in sems.values():
                p.w(e, s)

    eyeb = st.enter_context(nc.sbuf_tensor("eyeb", [128, 128], BF16))
    eyef = st.enter_context(nc.sbuf_tensor("eyef", [128, 128], F32))
    maskt = st.enter_context(nc.sbuf_tensor("maskt", [128, 128], F32))
    onesb = st.enter_context(nc.sbuf_tensor("onesb", [128, 129], BF16))
    onesf = st.enter_context(nc.sbuf_tensor("onesf", [128, 1], F32))
    maskub = st.enter_context(nc.sbuf_tensor("maskub", [128, 128], BF16))
    negalb = st.enter_context(nc.sbuf_tensor("negalb", [128, 128], BF16))
    epsv = st.enter_context(nc.sbuf_tensor("epsv", [128, 1], F32))
    stt = st.enter_context(nc.sbuf_tensor("stt", [128, 8], F32))
    sumsq = st.enter_context(nc.sbuf_tensor("sumsq", [128, 1], F32))

    p.o("g", "dma_start", eyeb[:], c_eyeb, inc=(dC, 16))
    p.o("g", "dma_start", eyef[:], c_eyef, inc=(dC, 16))
    p.o("g", "dma_start", maskt[:], c_mask, inc=(dC, 16))
    p.o("g", "dma_start", onesb[:], c_ones, inc=(dC, 16))
    p.o("g", "memset", onesf[:], 1.0, inc=(po, 1))
    p.o("g", "dma_start", maskub[:], c_masku, inc=(dC, 16))
    p.o("g", "dma_start", negalb[:], c_negall, inc=(dC, 16))
    p.o("g", "memset", epsv[:], EPS, inc=(po, 1))

    def ln_stats_apply(xf32, out_ap, junk, extra_out=None):
        p.w("v", ac)
        p.o("v", "tensor_reduce", stt[:, 0:1], xf32, AX.X, ALU.add, negate=True,
            cwait=True, cinc=True)
        p.o("v", "tensor_scalar", stt[:, 0:1], stt[:, 0:1], 1.0 / D, None,
            ALU.mult, inc=(ve, 1), cwait=True, cinc=True)
        p.w("s", ve)
        p.o("s", "activation", junk, xf32, AF.Square, accum_out=sumsq[:],
            inc=(ac, 1))
        p.w("v", ac)
        p.o("v", "tensor_tensor", stt[:, 1:2], stt[:, 0:1], stt[:, 0:1], ALU.mult,
            cwait=True, cinc=True)
        p.o("v", "tensor_scalar", stt[:, 2:3], sumsq[:], 1.0 / D, stt[:, 1:2],
            ALU.mult, ALU.subtract, inc=(ve, 1), cwait=True, cinc=True)
        p.w("s", ve)
        p.o("s", "activation", stt[:, 3:4], stt[:, 2:3], AF.Sqrt, bias=epsv[:],
            inc=(ac, 1), cwait=True, cinc=True)
        p.w("v", ac)
        p.o("v", "reciprocal", stt[:, 4:5], stt[:, 3:4], cwait=True, cinc=True)
        p.o("v", "tensor_tensor", stt[:, 5:6], stt[:, 0:1], stt[:, 4:5], ALU.mult,
            inc=(ve, 1), cwait=True, cinc=True)
        p.w("s", ve)
        p.o("s", "activation", out_ap, xf32, AF.Identity, bias=stt[:, 5:6],
            scale=stt[:, 4:5], inc=(ac, 1))
        if extra_out is not None:
            p.o("s", "activation", extra_out, xf32, AF.Identity, bias=stt[:, 5:6],
                scale=stt[:, 4:5], inc=(ac, 1))

    stAB = ExitStack()
    LN1T = stAB.enter_context(nc.sbuf_tensor("LN1T", [128, DC * T], BF16))

    if "A" in phases:
        stA = ExitStack()
        xa = stA.enter_context(nc.sbuf_tensor("xa", [128, 2 * D], F32))
        lnt = stA.enter_context(nc.sbuf_tensor("lnt", [128, D], BF16))
        junkA = stA.enter_context(nc.sbuf_tensor("junkA", [128, D], BF16))
        psA = [stA.enter_context(nc.psum_tensor(f"psA{i}", [128, 1024], BF16))
               for i in range(2)]
        xa_sl = Slot(2)
        psA_sl = Slot(2)

        for e in ENGS:
            p.w(e, dC)
            p.w(e, po)

        for i in range(NT):
            xa_sl.wait(p, "y", i)
            xav = xa[:, (i % 2) * D:(i % 2 + 1) * D]
            p.o("y", "dma_start", xav, x_in[128 * i:128 * (i + 1), :],
                inc=(rxA.s(i), 16))
            p.w("v", rxA.s(i))
            p.w("s", rxA.s(i))
            ln_stats_apply(xav, lnt[:], junkA[:])
            xa_sl.set(i, ac, ve)
            p.w("t", ac)
            for g in range(4):
                par = g % 2
                psA_sl.wait(p, "t", g + 4 * i)
                for j in range(8):
                    c = 8 * g + j
                    p.o("t", "transpose", psA[par][:, 128 * j:128 * (j + 1)],
                        lnt[:, 128 * c:128 * (c + 1)], eyeb[:],
                        inc=(pe, 1) if j == 7 else None)
                p.w("v", pe)
                seg = LN1T[:, 8 * g * T:8 * (g + 1) * T]
                seg3 = seg.rearrange("p (c t) -> p c t", c=8)
                p.o("v", "tensor_copy", seg3[:, :, 128 * i:128 * (i + 1)],
                    psA[par].rearrange("p (c t) -> p c t", c=8), inc=(ve, 1))
                psA_sl.set(g + 4 * i, ve)
            if DEBUG_SLOW:
                barrier()
        barrier()
        if "ln1t_dbg" in dbg and "B" not in phases:
            ln1t_dbg = dram("ln1t_dbg", [D, T], BF16, kind="ExternalOutput")
            for c in range(DC):
                p.o("y", "dma_start", ln1t_dbg[128 * c:128 * (c + 1), :],
                    LN1T[:, T * c:T * (c + 1)], inc=(dS, 16))
            barrier()
        stA.close()

    if "B" in phases:
        stB = ExitStack()
        bufs = {}

        def alloc_b(stk, with_vt):
            bufs["ropes"] = stk.enter_context(
                nc.sbuf_tensor("ropes" + str(with_vt), [128, 4 * T], BF16))
            bufs["wfm"] = stk.enter_context(
                nc.sbuf_tensor("wfm" + str(with_vt), [128, 2 * DC * 128], BF16))
            bufs["hbr"] = stk.enter_context(
                nc.sbuf_tensor("hbr" + str(with_vt), [128, T], BF16))
            bufs["hbs"] = stk.enter_context(
                nc.sbuf_tensor("hbs" + str(with_vt), [128, T], BF16))
            bufs["hbo"] = stk.enter_context(
                nc.sbuf_tensor("hbo" + str(with_vt), [128, 2 * T], BF16))
            bufs["psB"] = [stk.enter_context(
                nc.psum_tensor(f"psB{with_vt}{i}", [128, 512], F32))
                for i in range(2)]
            if with_vt:
                bufs["vtt"] = stk.enter_context(
                    nc.sbuf_tensor("vtt", [128, 2 * 512], BF16))
                bufs["psBT"] = [stk.enter_context(
                    nc.psum_tensor(f"psBT{i}", [128, 512], BF16))
                    for i in range(2)]
            p.o("g", "dma_start", bufs["ropes"][:], c_rope, inc=(dC2, 16))

        alloc_b(stB, 1)
        wfm_sl = Slot(2)
        psB_sl = Slot(2)
        hbr_sl = Slot(1)
        hbs_sl = Slot(1)
        hbo_sl = Slot(2)
        vtt_sl = Slot(2)
        psBT_sl = Slot(2)

        def emit_fm_chunk(wsrc, m, kc, mmi, rhs_tile):
            wfm, hbr, psB = bufs["wfm"], bufs["hbr"], bufs["psB"]
            par = mmi % 2
            wfm_sl.wait(p, "g", mmi)
            wsl = wsrc[:, 128 * m:128 * (m + 1)]
            wdst = wfm[:, par * DC * 128:par * DC * 128 + kc * 128]
            p.o("g", "dma_start", wdst.rearrange("p (k m) -> p k m", k=kc),
                wsl.rearrange("(k p) m -> p k m", p=128), inc=(rwB.s(mmi), 16))
            p.w("t", rwB.s(mmi))
            p.w("t", dC2)
            hbr_sl.wait(p, "s", 0)
            for n in range(N4):
                psB_sl.wait(p, "t", n)
                for k in range(kc):
                    p.o("t", "matmul", psB[n % 2][:],
                        wfm[:, par * DC * 128 + 128 * k:
                            par * DC * 128 + 128 * (k + 1)],
                        rhs_tile(k, n),
                        start=(k == 0), stop=(k == kc - 1),
                        inc=(pe, 1) if k == kc - 1 else None)
                p.w("s", pe)
                p.o("s", "copy", hbr[:, 512 * n:512 * (n + 1)], psB[n % 2][:],
                    inc=(ac, 1))
                psB_sl.set(n, ac)
            wfm_sl.set(mmi, pe)

        def emit_rope_store(m, ci, dest, mmi):
            hbr, hbs, hbo, ropes = (bufs["hbr"], bufs["hbs"], bufs["hbo"],
                                    bufs["ropes"])
            hbs_sl.wait(p, "y", 0)
            p.w("y", ac)
            p.o("y", "dma_start", hbs[0:64, :], hbr[64:128, :], inc=(dR, 16))
            p.o("y", "dma_start", hbs[64:128, :], hbr[0:64, :], inc=(dR, 16))
            p.w("v", dR)
            p.w("v", ac)
            p.w("v", dC2)
            hbo_sl.wait(p, "v", mmi)
            ob = hbo[:, (mmi % 2) * T:(mmi % 2 + 1) * T]
            p.o("v", "tensor_tensor", ob, hbr[:], ropes[:, ci * T:(ci + 1) * T],
                ALU.mult, cwait=True, cinc=True)
            p.o("v", "tensor_tensor", hbs[:], hbs[:],
                ropes[:, (ci + 1) * T:(ci + 2) * T], ALU.mult, cinc=True)
            p.o("v", "tensor_tensor", ob, ob, hbs[:], ALU.add, inc=(ve, 1),
                cwait=True, cinc=True)
            hbr_sl.set(0, ve)
            hbs_sl.set(0, ve)
            p.w("y", ve)
            p.o("y", "dma_start", dest[128 * m:128 * (m + 1), :], ob,
                inc=(rsB.s(mmi), 16))
            hbo_sl.set(mmi, rsB.s(mmi))

        def emit_plain_store(m, dest):
            p.w("y", ac)
            p.o("y", "dma_start", dest[128 * m:128 * (m + 1), :], bufs["hbr"][:],
                inc=(dP, 16))
            hbr_sl.set(0, dP)

        def emit_tm_store(m, dest, mmi):
            hbr, vtt, psBT = bufs["hbr"], bufs["vtt"], bufs["psBT"]
            for n in range(N4):
                tpar = (mmi * N4 + n) % 2
                p.w("t", ac)
                psBT_sl.wait(p, "t", mmi * N4 + n)
                for j in range(4):
                    p.o("t", "transpose", psBT[tpar][:, 128 * j:128 * (j + 1)],
                        hbr[:, 512 * n + 128 * j:512 * n + 128 * (j + 1)],
                        eyeb[:], inc=(pe, 1) if j == 3 else None)
                p.w("v", pe)
                vtt_sl.wait(p, "v", mmi * N4 + n)
                vtv = vtt[:, tpar * 512:(tpar + 1) * 512]
                p.o("v", "tensor_copy", vtv, psBT[tpar][:], inc=(ve, 1))
                psBT_sl.set(mmi * N4 + n, ve)
                p.w("y", ve)
                dseg = dest[512 * n:512 * (n + 1), 128 * m:128 * (m + 1)]
                p.o("y", "dma_start", dseg.rearrange("(j p) f -> p j f", p=128),
                    vtv.rearrange("p (j f) -> p j f", j=4),
                    inc=(rsV.s(mmi * N4 + n), 16))
                vtt_sl.set(mmi * N4 + n, rsV.s(mmi * N4 + n))
            hbr_sl.set(0, pe)

        def ln1_rhs(k, n):
            return LN1T[:, T * k + 512 * n:T * k + 512 * (n + 1)]

        mmi = 0
        for (wsrc, MCH, kind, dest) in [
                (wq_d, NH, "rope_q", QT),
                (wk_d, NKV, "rope_k", KT),
                (wlq_d, NQC, "plain", LQT),
                (wv_d, NKV, "tm", VT),
                (wlv_d, LH, "tm", LVT)]:
            for m in range(MCH):
                emit_fm_chunk(wsrc, m, DC, mmi, ln1_rhs)
                if kind == "rope_q":
                    emit_rope_store(m, 0, dest, mmi)
                elif kind == "rope_k":
                    emit_rope_store(m, 2, dest, mmi)
                elif kind == "plain":
                    emit_plain_store(m, dest)
                else:
                    emit_tm_store(m, dest, mmi)
                mmi += 1

        barrier()
        if "ln1t_dbg" in dbg:
            ln1t_dbg = dram("ln1t_dbg", [D, T], BF16, kind="ExternalOutput")
            for c in range(DC):
                p.o("y", "dma_start", ln1t_dbg[128 * c:128 * (c + 1), :],
                    LN1T[:, T * c:T * (c + 1)], inc=(dS, 16))
            barrier()
        stB.close()
        stAB.close()
        stB = ExitStack()
        alloc_b(stB, 0)
        wfm_sl = Slot(2)
        psB_sl = Slot(2)
        hbr_sl = Slot(1)
        hbs_sl = Slot(1)
        hbo_sl = Slot(2)
        lqs = stB.enter_context(nc.sbuf_tensor("lqs", [128, NQC * T], BF16))
        for k in range(NQC):
            p.o("y", "dma_start", lqs[:, T * k:T * (k + 1)],
                LQT[128 * k:128 * (k + 1), :], inc=(dX, 16))
        p.w("t", dX)

        def lq_rhs(k, n):
            return lqs[:, T * k + 512 * n:T * k + 512 * (n + 1)]

        for m in range(LH):
            emit_fm_chunk(wlk_d, m, NQC, mmi, lq_rhs)
            emit_rope_store(m, 2, LKT, mmi)
            mmi += 1

        barrier()
        stB.close()
    else:
        stAB.close()

    if "C" in phases:
        stC = ExitStack()
        qh = stC.enter_context(nc.sbuf_tensor("qh", [128, 4 * T], BF16))
        kh = stC.enter_context(nc.sbuf_tensor("kh", [128, 2 * T], BF16))
        lkh = stC.enter_context(nc.sbuf_tensor("lkh", [128, 4 * T], BF16))
        vh = stC.enter_context(nc.sbuf_tensor("vh", [128, 2 * T], BF16))
        lvh = stC.enter_context(nc.sbuf_tensor("lvh", [128, 4 * T], BF16))
        PT = stC.enter_context(nc.sbuf_tensor("PT", [128, 2 * 1024], BF16))
        ohb = stC.enter_context(nc.sbuf_tensor("ohb", [128, 4 * T], BF16))
        sumall = stC.enter_context(nc.sbuf_tensor("sumall", [1, 4 * T], F32))
        recb = stC.enter_context(nc.sbuf_tensor("recb", [1, T], BF16))
        SA = [stC.enter_context(nc.psum_tensor(f"SA{i}", [128, 1024], F32))
              for i in range(2)]
        OA = [stC.enter_context(nc.psum_tensor(f"OA{i}", [128, 512], F32))
              for i in range(2)]
        SU = [stC.enter_context(nc.psum_tensor(f"SU{i}", [1, 512], F32))
              for i in range(2)]
        SA_sl = Slot(2)
        PT_sl = Slot(2)
        OA_sl = Slot(2)
        SU_sl = Slot(2)
        RB_sl = Slot(1)
        oh_sl = Slot(4)
        pair_done = {}

        def attn_quad(qvs, kbs, vbs, rows):
            sstep = 0
            ce_hist = {}
            for m in range(N4):
                for u in range(4):
                    hh = u // 2
                    qv = qvs[hh]
                    ktiles = 4 * m + 4
                    kpairs = ktiles // 2
                    useq = 4 * m + u
                    oa = OA[u % 2]
                    su = SU[u % 2]
                    OA_sl.wait(p, "t", useq)
                    SU_sl.wait(p, "t", useq)
                    for t in range(kpairs + 1):
                        if t < kpairs:
                            sa = SA[sstep % 2]
                            SA_sl.wait(p, "t", sstep)
                            for half in range(2):
                                kt = 2 * t + half
                                o = kt - 4 * m
                                sav = sa[:, 512 * half:512 * (half + 1)]
                                lastmm = (half == 1) and (o < 0)
                                p.o("t", "matmul", sav,
                                    kbs[u][:, 128 * kt:128 * (kt + 1)],
                                    qv[:, 512 * m:512 * (m + 1)],
                                    start=True, stop=(o < 0),
                                    inc=(cS, 1) if lastmm else None)
                                if o >= 0:
                                    for j in range(o + 1):
                                        lt = negalb if j < o else maskub
                                        p.o("t", "matmul",
                                            sav[:, 128 * j:128 * (j + 1)],
                                            lt[:], eyeb[:], start=False,
                                            stop=(j == o),
                                            inc=(cS, 1) if (j == o and half == 1)
                                            else None)
                            p.w("s", cS)
                            PT_sl.wait(p, "s", sstep)
                            ptv = PT[:, (sstep % 2) * 1024:(sstep % 2 + 1) * 1024]
                            p.o("s", "activation", ptv, sa[:], AF.Exp,
                                inc=(cE, 1))
                            ce_hist[sstep] = cE.t
                            SA_sl.set(sstep, cE)
                        if t >= 1:
                            pstep = sstep - 1
                            p.w("t", cE, ce_hist[pstep])
                            for half in range(2):
                                kt = 2 * (t - 1) + half
                                pv_ = PT[:, (pstep % 2) * 1024 + 512 * half:
                                         (pstep % 2) * 1024 + 512 * (half + 1)]
                                p.o("t", "matmul", oa[:],
                                    vbs[u][:, 128 * kt:128 * (kt + 1)], pv_,
                                    start=(kt == 0), stop=(kt == ktiles - 1),
                                    inc=(cPV, 1) if kt == ktiles - 1 else None)
                                p.o("t", "matmul", su[0:1, :], onesb[:, 0:1],
                                    pv_,
                                    start=(kt == 0), stop=(kt == ktiles - 1),
                                    inc=(pe, 1) if half == 1 else None)
                            PT_sl.set(pstep, pe)
                        if t < kpairs:
                            sstep += 1
                    p.w("v", cPV)
                    if m == 0:
                        oh_sl.wait(p, "v", u)
                    p.o("v", "tensor_copy",
                        ohb[:, u * T + 512 * m:u * T + 512 * (m + 1)],
                        oa[:], inc=(cOe, 1))
                    OA_sl.set(useq, cOe)
                    p.w("v", pe)
                    p.o("v", "tensor_copy",
                        sumall[0:1, u * T + 512 * m:u * T + 512 * (m + 1)],
                        su[0:1, :], inc=(cTeV, 1))
                    SU_sl.set(useq, cTeV)
            for u in range(4):
                p.o("v", "reciprocal", sumall[0:1, u * T:(u + 1) * T],
                    sumall[0:1, u * T:(u + 1) * T], cwait=True, cinc=True)
                p.w("s", ve)
                p.o("s", "copy", recb[0:1, :], sumall[0:1, u * T:(u + 1) * T],
                    inc=(ac, 1))
                for c in range(N4):
                    RB_sl.wait(p, "t", 0)
                    p.w("t", ac)
                    p.w("t", cE)
                    rbv = SA[0][:, 0:512]
                    p.o("t", "matmul", rbv, onesb[0:1, 0:128],
                        recb[0:1, 512 * c:512 * (c + 1)],
                        start=True, stop=True, inc=(cT, 1))
                    p.w("v", cT)
                    p.w("v", cOe)
                    p.o("v", "tensor_tensor",
                        ohb[:, u * T + 512 * c:u * T + 512 * (c + 1)],
                        ohb[:, u * T + 512 * c:u * T + 512 * (c + 1)],
                        rbv, ALU.mult, inc=(ve, 1), cwait=True)
                    RB_sl.set(0, ve)
                p.w("y", ve)
                p.o("y", "dma_start",
                    AOT[128 * rows[u]:(128 * (rows[u] + 1)), :],
                    ohb[:, u * T:(u + 1) * T], inc=(rsO.s(u), 16))
                oh_sl.set(u, rsO.s(u))
            SA_sl.set(0, ve)

        for h0 in range(0, NH, 2):
            kvi = h0 // 4
            pp = (h0 // 2) % 2
            guards = pair_done.get(h0 - 4, [])
            p.wg("y", guards)
            for hh in (0, 1):
                h = h0 + hh
                b = 2 * pp + hh
                p.o("y", "dma_start", qh[:, b * T:(b + 1) * T],
                    QT[128 * h:128 * (h + 1), :], inc=(rq.s(b), 16))
                p.o("y", "dma_start", lkh[:, b * T:(b + 1) * T],
                    LKT[128 * h:128 * (h + 1), :], inc=(rl.s(b), 16))
                lvseg = LVT[:, 128 * h:128 * (h + 1)]
                lvv = lvh[:, b * T:(b + 1) * T]
                p.o("y", "dma_start", lvv.rearrange("p (t c) -> p t c", t=NT),
                    lvseg.rearrange("(t p) c -> p t c", p=128),
                    inc=(rl.s(b), 16))
            do_k0 = (h0 % 4 == 0)
            if do_k0:
                p.wg("y", pair_done.get(h0 - 8, []))
                p.o("y", "dma_start", kh[:, (kvi % 2) * T:(kvi % 2 + 1) * T],
                    KT[128 * kvi:128 * (kvi + 1), :], inc=(rk.s(kvi), 16))
                vseg = VT[:, 128 * kvi:128 * (kvi + 1)]
                vhv = vh[:, (kvi % 2) * T:(kvi % 2 + 1) * T]
                p.o("y", "dma_start", vhv.rearrange("p (t c) -> p t c", t=NT),
                    vseg.rearrange("(t p) c -> p t c", p=128),
                    inc=(rk.s(kvi), 16))
            for e in ("t", "s"):
                p.w(e, rq.s(2 * pp)); p.w(e, rq.s(2 * pp + 1))
                p.w(e, rk.s(kvi))
                p.w(e, rl.s(2 * pp)); p.w(e, rl.s(2 * pp + 1))
            kvv = kh[:, (kvi % 2) * T:(kvi % 2 + 1) * T]
            vvv = vh[:, (kvi % 2) * T:(kvi % 2 + 1) * T]
            qvs = (qh[:, 2 * pp * T:(2 * pp + 1) * T],
                   qh[:, (2 * pp + 1) * T:(2 * pp + 2) * T])
            kbs = (kvv, lkh[:, 2 * pp * T:(2 * pp + 1) * T], kvv,
                   lkh[:, (2 * pp + 1) * T:(2 * pp + 2) * T])
            vbs = (vvv, lvh[:, 2 * pp * T:(2 * pp + 1) * T], vvv,
                   lvh[:, (2 * pp + 1) * T:(2 * pp + 2) * T])
            rows = (h0, NH + h0, h0 + 1, NH + h0 + 1)
            attn_quad(qvs, kbs, vbs, rows)
            pair_done[h0] = [(cS, cS.t), (cPV, cPV.t), (ac, ac.t),
                             (ve, ve.t), (pe, pe.t)]

        barrier()
        stC.close()

    if "D" in phases:
        stD1 = ExitStack()
        rhsD = stD1.enter_context(nc.sbuf_tensor("rhsD", [128, 32 * T], BF16))
        wodb = stD1.enter_context(nc.sbuf_tensor("wodb", [128, 2 * 32 * 128], BF16))
        aob = stD1.enter_context(nc.sbuf_tensor("aob", [128, 2 * 512], F32))
        psD1 = [stD1.enter_context(nc.psum_tensor(f"psD1{i}", [128, 512], F32))
                for i in range(2)]
        wod_sl = Slot(2)
        aob_sl = Slot(2)
        psD_sl = Slot(2)

        for kg in range(2):
            p.w("y", pe)
            for r in range(8):
                seg = AOT[4096 * kg + 512 * r:4096 * kg + 512 * (r + 1), :]
                p.o("y", "dma_start",
                    rhsD[:, 4 * r * T:4 * (r + 1) * T].rearrange(
                        "p (k t) -> p k t", k=4),
                    seg.rearrange("(k p) t -> p k t", p=128), inc=(dX, 16))
            for m in range(DC):
                par = m % 2
                wod_sl.wait(p, "g", m)
                wsl = wolo_d[4096 * kg:4096 * (kg + 1), 128 * m:128 * (m + 1)]
                wdst = wodb[:, par * 32 * 128:(par + 1) * 32 * 128]
                p.o("g", "dma_start", wdst.rearrange("p (k m) -> p k m", k=32),
                    wsl.rearrange("(k p) m -> p k m", p=128),
                    inc=(rwD.s(m), 16))
                p.w("t", rwD.s(m))
                if m == 0:
                    p.w("t", dX)
                for n in range(N4):
                    psD_sl.wait(p, "t", n)
                    for k in range(32):
                        p.o("t", "matmul", psD1[n % 2][:],
                            wodb[:, par * 32 * 128 + 128 * k:
                                 par * 32 * 128 + 128 * (k + 1)],
                            rhsD[:, k * T + 512 * n:k * T + 512 * (n + 1)],
                            start=(k == 0), stop=(k == 31),
                            inc=(pe, 1) if k == 31 else None)
                    p.w("s", pe)
                    aob_sl.wait(p, "s", n)
                    av = aob[:, (n % 2) * 512:(n % 2 + 1) * 512]
                    p.o("s", "copy", av, psD1[n % 2][:], inc=(ac, 1))
                    psD_sl.set(n, ac)
                    p.w("y", ac)
                    p.o("y", "dma_start",
                        AOP[kg, 128 * m:128 * (m + 1), 512 * n:512 * (n + 1)],
                        av, inc=(rsA1.s(n), 16))
                    aob_sl.set(n, rsA1.s(n))
                wod_sl.set(m, pe)
        barrier()
        stD1.close()

    stDE = ExitStack()
    XFT = stDE.enter_context(nc.sbuf_tensor("XFT", [128, NQC * T], BF16))

    if "D" in phases:
        stD2 = ExitStack()
        pbuf = stD2.enter_context(nc.sbuf_tensor("pbuf", [128, 2 * D], F32))
        accD = stD2.enter_context(nc.sbuf_tensor("accD", [128, D], F32))
        atok = stD2.enter_context(nc.sbuf_tensor("atok", [128, D], F32))
        xrD = stD2.enter_context(nc.sbuf_tensor("xrD", [128, D], F32))
        ln2b = stD2.enter_context(nc.sbuf_tensor("ln2b", [128, D], BF16))
        xffold = stD2.enter_context(nc.sbuf_tensor("xffold", [128, LAT], F32))
        xfb = stD2.enter_context(nc.sbuf_tensor("xfb", [128, LAT], BF16))
        junkD = stD2.enter_context(nc.sbuf_tensor("junkD", [128, D], BF16))
        l2st = stD2.enter_context(nc.sbuf_tensor("l2st", [128, 2 * 1024], BF16))
        psDT = [stD2.enter_context(nc.psum_tensor(f"psDT{i}", [128, 512], F32))
                for i in range(2)]
        psDB = [stD2.enter_context(nc.psum_tensor(f"psDB{i}", [128, 1024], BF16))
                for i in range(2)]
        pb_sl = Slot(2)
        psDT_sl = Slot(2)
        psDB_sl = Slot(2)
        l2st_sl = Slot(2)
        xrD_sl = Slot(1)

        for ti in range(NT):
            for kg in range(2):
                pb_sl.wait(p, "y", 2 * ti + kg)
                pv = pbuf[:, (kg % 2) * D:(kg % 2 + 1) * D]
                seg = AOP[kg, :, 128 * ti:128 * (ti + 1)]
                p.o("y", "dma_start", pv.rearrange("p (c t) -> p c t", c=DC),
                    seg.rearrange("(c p) t -> p c t", p=128),
                    inc=(rpD.s(2 * ti + kg), 16))
            p.w("v", rpD.s(2 * ti))
            p.w("v", rpD.s(2 * ti + 1))
            p.w("v", ac)
            p.w("v", pe)
            p.o("v", "tensor_tensor", accD[:],
                pbuf[:, 0:D], pbuf[:, D:2 * D], ALU.add,
                inc=(ve, 1), cwait=True, cinc=True)
            pb_sl.set(2 * ti, ve)
            pb_sl.set(2 * ti + 1, ve)
            xrD_sl.wait(p, "y", ti)
            p.o("y", "dma_start", xrD[:], x_in[128 * ti:128 * (ti + 1), :],
                inc=(rxD.s(ti), 16))
            p.w("t", ve)
            for g in range(8):
                par = g % 2
                psDT_sl.wait(p, "t", g)
                for j in range(4):
                    c = 4 * g + j
                    p.o("t", "transpose", psDT[par][:, 128 * j:128 * (j + 1)],
                        accD[:, 128 * c:128 * (c + 1)], eyef[:],
                        inc=(pe, 1) if j == 3 else None)
                p.w("v", pe)
                p.o("v", "tensor_copy", atok[:, 512 * g:512 * (g + 1)],
                    psDT[par][:], inc=(ve, 1))
                psDT_sl.set(g, ve)
            p.w("s", ve)
            p.w("v", ve)
            ln_stats_apply(atok[:], atok[:], junkD[:])
            p.w("v", ac)
            p.w("v", rxD.s(ti))
            p.o("v", "tensor_tensor", xrD[:], xrD[:], atok[:], ALU.add,
                inc=(ve, 1), cwait=True, cinc=True)
            p.w("y", ve)
            p.o("y", "dma_start", X2[128 * ti:128 * (ti + 1), :], xrD[:],
                inc=(rsX2.s(ti), 16))
            ln_stats_apply(xrD[:], atok[:], junkD[:])
            xrD_sl.set(ti, ac, rsX2.s(ti))
            p.w("v", ac)
            p.o("v", "tensor_copy", ln2b[:], atok[:], cwait=True, cinc=True)
            p.w("v", ac)
            p.o("v", "tensor_tensor", xffold[:], atok[:, 0:LAT],
                atok[:, LAT:2 * LAT], ALU.add, cwait=True, cinc=True)
            p.o("v", "tensor_tensor", xffold[:], xffold[:],
                atok[:, 2 * LAT:3 * LAT], ALU.add, cwait=True, cinc=True)
            p.o("v", "tensor_tensor", xffold[:], xffold[:],
                atok[:, 3 * LAT:4 * LAT], ALU.add, cwait=True, cinc=True)
            p.o("v", "tensor_copy", xfb[:], xffold[:], inc=(ve, 1), cwait=True)
            p.w("t", ac)
            p.w("t", ve)
            for g in range(4):
                par = g % 2
                psDB_sl.wait(p, "t", g)
                for j in range(8):
                    c = 8 * g + j
                    p.o("t", "transpose", psDB[par][:, 128 * j:128 * (j + 1)],
                        ln2b[:, 128 * c:128 * (c + 1)], eyeb[:],
                        inc=(pe, 1) if j == 7 else None)
                p.w("v", pe)
                l2st_sl.wait(p, "v", 4 * ti + g)
                lsv = l2st[:, (g % 2) * 1024:(g % 2 + 1) * 1024]
                p.o("v", "tensor_copy", lsv, psDB[par][:], inc=(ve, 1))
                psDB_sl.set(g, ve)
                p.w("y", ve)
                seg = LN2T[128 * 8 * g:128 * 8 * (g + 1),
                           128 * ti:128 * (ti + 1)]
                p.o("y", "dma_start", seg.rearrange("(c p) t -> p c t", p=128),
                    lsv.rearrange("p (c t) -> p c t", c=8),
                    inc=(rsL.s(4 * ti + g), 16))
                l2st_sl.set(4 * ti + g, rsL.s(4 * ti + g))
            psDB_sl.wait(p, "t", 4)
            for j in range(8):
                p.o("t", "transpose", psDB[0][:, 128 * j:128 * (j + 1)],
                    xfb[:, 128 * j:128 * (j + 1)], eyeb[:],
                    inc=(pe, 1) if j == 7 else None)
            p.w("v", pe)
            segx = XFT.rearrange("p (c t) -> p c t", c=NQC)
            p.o("v", "tensor_copy", segx[:, :, 128 * ti:128 * (ti + 1)],
                psDB[0].rearrange("p (c t) -> p c t", c=8), inc=(ve, 1))
            psDB_sl.set(4, ve)

        barrier()
        stD2.close()

    if "E" in phases:
        stE = ExitStack()
        a13b = stE.enter_context(nc.sbuf_tensor("a13b", [128, DC * 4], BF16))
        b1b = stE.enter_context(nc.sbuf_tensor("b1b", [2, HF], BF16))
        b3b = stE.enter_context(nc.sbuf_tensor("b3b", [2, HF], BF16))
        lnsE = stE.enter_context(nc.sbuf_tensor("lnsE", [128, 2 * 512], BF16))
        xgf = stE.enter_context(nc.sbuf_tensor("xgf", [2, T], F32))
        xuf = stE.enter_context(nc.sbuf_tensor("xuf", [2, T], F32))
        xgb = stE.enter_context(nc.sbuf_tensor("xgb", [2, T], BF16))
        xub = stE.enter_context(nc.sbuf_tensor("xub", [2, T], BF16))
        w13 = stE.enter_context(nc.sbuf_tensor("w13", [128, 2 * 16 * 128], BF16))
        sgb = stE.enter_context(nc.sbuf_tensor("sgb", [128, 2 * 512], BF16))
        upb = stE.enter_context(nc.sbuf_tensor("upb", [128, 2 * 512], BF16))
        gur = stE.enter_context(nc.sbuf_tensor("gur", [128, 2 * T], BF16))
        psE = [stE.enter_context(nc.psum_tensor(f"psE{i}", [128, 512], F32))
               for i in range(4)]
        psEA1 = stE.enter_context(nc.psum_tensor("psEA1", [2, 512], F32))
        psEA2 = stE.enter_context(nc.psum_tensor("psEA2", [2, 512], F32))
        lnsE_sl = Slot(2)
        w13_sl = Slot(2)
        gur_sl = Slot(2)

        p.o("g", "dma_start", a13b[:].rearrange("p (c r) -> p c r", c=DC),
            a13t_d.rearrange("(c p) r -> p c r", p=128), inc=(dC2, 16))
        p.o("g", "dma_start", b1b[:], b1t_d, inc=(dC2, 16))
        p.o("g", "dma_start", b3b[:], b3t_d, inc=(dC2, 16))

        p.w("t", dC2)
        for n in range(N4):
            p.w("t", ve)
            for k in range(DC):
                ci = n * DC + k
                lnsE_sl.wait(p, "y", ci)
                lv = lnsE[:, (ci % 2) * 512:(ci % 2 + 1) * 512]
                p.o("y", "dma_start", lv,
                    LN2T[128 * k:128 * (k + 1), 512 * n:512 * (n + 1)],
                    inc=(rlE.s(ci), 16))
                p.w("t", rlE.s(ci))
                p.o("t", "matmul", psEA1[0:2, :], a13b[:, 4 * k:4 * k + 2],
                    lv, start=(k == 0), stop=(k == DC - 1),
                    inc=None)
                p.o("t", "matmul", psEA2[0:2, :], a13b[:, 4 * k + 2:4 * k + 4],
                    lv, start=(k == 0), stop=(k == DC - 1),
                    inc=(pe, 1))
                lnsE_sl.set(ci, pe)
            p.w("v", pe)
            p.o("v", "tensor_copy", xgf[:, 512 * n:512 * (n + 1)],
                psEA1[0:2, :])
            p.o("v", "tensor_copy", xuf[:, 512 * n:512 * (n + 1)],
                psEA2[0:2, :], inc=(ve, 1))
        p.w("s", ve)
        p.o("s", "mul", xgb[:], xgf[:], 0.5)
        p.o("s", "mul", xub[:], xuf[:], 0.5, inc=(ac, 1))
        xa13_guard = [(ac, ac.t)]
        psE_sl = Slot(2)
        sg_sl = Slot(2)

        for m in range(HC):
            par = m % 2
            w13_sl.wait(p, "g", m)
            wdst1 = w13[:, par * 16 * 128:par * 16 * 128 + NQC * 128]
            wdst3 = w13[:, par * 16 * 128 + NQC * 128:(par + 1) * 16 * 128]
            p.o("g", "dma_start", wdst1.rearrange("p (k m) -> p k m", k=NQC),
                w1t_d[:, 128 * m:128 * (m + 1)].rearrange("(k p) m -> p k m", p=128),
                inc=(rwE.s(m), 16))
            p.o("g", "dma_start", wdst3.rearrange("p (k m) -> p k m", k=NQC),
                w3t_d[:, 128 * m:128 * (m + 1)].rearrange("(k p) m -> p k m", p=128),
                inc=(rwE.s(m), 16))
            p.w("t", rwE.s(m))
            p.wg("t", xa13_guard)
            for n in range(N4):
                gp = psE[2 * (n % 2)]
                up = psE[2 * (n % 2) + 1]
                psE_sl.wait(p, "t", n)
                for k in range(NQC):
                    p.o("t", "matmul", gp[:],
                        w13[:, par * 16 * 128 + 128 * k:
                            par * 16 * 128 + 128 * (k + 1)],
                        XFT[:, T * k + 512 * n:T * k + 512 * (n + 1)],
                        start=(k == 0), stop=False)
                p.o("t", "matmul", gp[:], b1b[0:2, 128 * m:128 * (m + 1)],
                    xgb[0:2, 512 * n:512 * (n + 1)], start=False, stop=True,
                    inc=(pe, 1))
                for k in range(NQC):
                    p.o("t", "matmul", up[:],
                        w13[:, par * 16 * 128 + NQC * 128 + 128 * k:
                            par * 16 * 128 + NQC * 128 + 128 * (k + 1)],
                        XFT[:, T * k + 512 * n:T * k + 512 * (n + 1)],
                        start=(k == 0), stop=False)
                p.o("t", "matmul", up[:], b3b[0:2, 128 * m:128 * (m + 1)],
                    xub[0:2, 512 * n:512 * (n + 1)], start=False, stop=True,
                    inc=(pe, 1))
                p.w("s", pe)
                sg_sl.wait(p, "s", n)
                sgv = sgb[:, (n % 2) * 512:(n % 2 + 1) * 512]
                upv = upb[:, (n % 2) * 512:(n % 2 + 1) * 512]
                p.o("s", "activation", sgv, gp[:], AF.Silu)
                p.o("s", "copy", upv, up[:], inc=(ac, 1))
                psE_sl.set(n, ac)
                p.w("v", ac)
                gur_sl.wait(p, "v", m)
                gv = gur[:, (m % 2) * T + 512 * n:(m % 2) * T + 512 * (n + 1)]
                p.o("v", "tensor_tensor", gv, sgv, upv, ALU.mult,
                    inc=(ve, 1))
                sg_sl.set(n, ve)
            w13_sl.set(m, pe)
            p.w("y", ve)
            p.o("y", "dma_start", GU[128 * m:128 * (m + 1), :],
                gur[:, (m % 2) * T:(m % 2 + 1) * T], inc=(rsG.s(m), 16))
            gur_sl.set(m, rsG.s(m))

        barrier()
        stE.close()
    stDE.close()

    if "F" in phases:
        stF = ExitStack()
        yfa = stF.enter_context(nc.sbuf_tensor("yfa", [128, YC * 512], F32))
        yfb = stF.enter_context(nc.sbuf_tensor("yfb", [128, YC * 512], BF16))
        gub = stF.enter_context(nc.sbuf_tensor("gub", [128, 3 * 512], BF16))
        wd2 = stF.enter_context(nc.sbuf_tensor("wd2", [128, 2 * YC * 128], BF16))
        a2sb = stF.enter_context(nc.sbuf_tensor("a2sb", [128, 2 * 88], BF16))
        b2sb = stF.enter_context(nc.sbuf_tensor("b2sb", [2, D], BF16))
        xa2f = stF.enter_context(nc.sbuf_tensor("xa2f", [2, 512], F32))
        xa2b = stF.enter_context(nc.sbuf_tensor("xa2b", [2, 512], BF16))
        fof = stF.enter_context(nc.sbuf_tensor("fof", [128, 2 * 512], F32))
        ftok = stF.enter_context(nc.sbuf_tensor("ftok", [128, 4 * D], F32))
        xrF = stF.enter_context(nc.sbuf_tensor("xrF", [128, 2 * D], F32))
        junkF = stF.enter_context(nc.sbuf_tensor("junkF", [128, D], BF16))
        psF = [stF.enter_context(nc.psum_tensor(f"psF{i}", [128, 512], F32))
               for i in range(2)]
        psFT = [stF.enter_context(nc.psum_tensor(f"psFT{i}", [128, 512], F32))
                for i in range(2)]
        psFA = stF.enter_context(nc.psum_tensor("psFA", [2, 512], F32))
        gub_sl = Slot(3)
        wd2_sl = Slot(2)
        psF_sl = Slot(2)
        fof_sl = Slot(2)
        psFT_sl = Slot(2)
        xr_sl = Slot(2)

        p.o("g", "dma_start", a2sb[:], a2r_d, inc=(dC2, 16))
        p.o("g", "dma_start", b2sb[:], b2t_d, inc=(dC2, 16))
        p.w("t", dC2)

        gu_tiles = []
        for j in range(YC):
            R = 128 if j < YC - 1 else 64
            for r in range(4):
                gu_tiles.append((j, r, YF * r + 128 * j, R))

        NTl = len(gu_tiles)
        for n in range(N4):
            p.w("t", ve)
            for c, (j, r, off, R) in enumerate(gu_tiles):
                ci = n * NTl + c
                gub_sl.wait(p, "y", ci)
                gv = gub[0:R, (c % 3) * 512:(c % 3 + 1) * 512]
                p.o("y", "dma_start", gv,
                    GU[off:off + R, 512 * n:512 * (n + 1)],
                    inc=(rgF.s(ci), 16))
                p.w("t", rgF.s(ci))
                p.o("t", "matmul", psFA[0:2, :], a2sb[0:R, 2 * c:2 * (c + 1)],
                    gv, start=(c == 0), stop=(c == NTl - 1),
                    inc=(pe, 1))
                p.w("v", rgF.s(ci))
                dst = yfa[0:R, 512 * j:512 * (j + 1)]
                if r == 0:
                    p.o("v", "tensor_copy", dst, gv, inc=(ve, 1), cwait=True,
                        cinc=True)
                else:
                    p.o("v", "tensor_tensor", dst, dst, gv, ALU.add,
                        inc=(ve, 1), cwait=True, cinc=True)
                gub_sl.set(ci, ve, pe)
            p.w("v", ve)
            p.o("v", "tensor_copy", yfb[:, 0:(YC - 1) * 512],
                yfa[:, 0:(YC - 1) * 512], cwait=True, cinc=True)
            p.o("v", "tensor_copy", yfb[0:64, (YC - 1) * 512:YC * 512],
                yfa[0:64, (YC - 1) * 512:YC * 512], inc=(ve, 1), cwait=True)
            p.w("v", pe)
            p.o("v", "tensor_copy", xa2f[:], psFA[0:2, :], inc=(ve, 1))
            p.w("s", ve)
            p.o("s", "mul", xa2b[:], xa2f[:], 0.5, inc=(ac, 1))
            blk_guard = [(ac, ac.t), (ve, ve.t)]
            for m in range(DC):
                par = m % 2
                wd2_sl.wait(p, "g", m)
                wv_ = wd2[:, par * YC * 128:(par + 1) * YC * 128]
                p.o("g", "dma_start",
                    wv_[:, 0:21 * 128].rearrange("p (k m) -> p k m", k=21),
                    w2t_d[0:2688, 128 * m:128 * (m + 1)].rearrange(
                        "(k p) m -> p k m", p=128), inc=(rwF.s(m), 16))
                p.o("g", "dma_start",
                    wv_[0:64, 21 * 128:22 * 128],
                    w2t_d[2688:2752, 128 * m:128 * (m + 1)], inc=(rwF.s(m), 16))
                p.w("t", rwF.s(m))
                p.wg("t", blk_guard)
                pp = psF[m % 2]
                psF_sl.wait(p, "t", m)
                for k in range(21):
                    p.o("t", "matmul", pp[:],
                        wv_[:, 128 * k:128 * (k + 1)],
                        yfb[:, 512 * k:512 * (k + 1)],
                        start=(k == 0), stop=False)
                p.o("t", "matmul", pp[:], wv_[0:64, 21 * 128:22 * 128],
                    yfb[0:64, 21 * 512:22 * 512], start=False, stop=False)
                p.o("t", "matmul", pp[:], b2sb[0:2, 128 * m:128 * (m + 1)],
                    xa2b[0:2, :], start=False, stop=True, inc=(pe, 1))
                wd2_sl.set(m, pe)
                p.w("s", pe)
                fof_sl.wait(p, "s", m)
                fv = fof[:, (m % 2) * 512:(m % 2 + 1) * 512]
                p.o("s", "copy", fv, pp[:], inc=(ac, 1))
                psF_sl.set(m, ac)
                p.w("t", ac)
                psFT_sl.wait(p, "t", m)
                for j in range(4):
                    p.o("t", "transpose", psFT[m % 2][:, 128 * j:128 * (j + 1)],
                        fv[:, 128 * j:128 * (j + 1)], eyef[:],
                        inc=(pe, 1) if j == 3 else None)
                fof_sl.set(m, pe)
                p.w("v", pe)
                p.o("v", "tensor_copy",
                    ftok.rearrange("p (j f) -> p j f", j=4)[:, :, 128 * m:128 * (m + 1)],
                    psFT[m % 2].rearrange("p (j f) -> p j f", j=4), inc=(ve, 1))
                psFT_sl.set(m, ve)
            for j in range(4):
                ti = 4 * n + j
                xr_sl.wait(p, "y", ti)
                xv = xrF[:, (j % 2) * D:(j % 2 + 1) * D]
                p.o("y", "dma_start", xv, X2[128 * ti:128 * (ti + 1), :],
                    inc=(rxF.s(ti), 16))
                ft = ftok[:, j * D:(j + 1) * D]
                p.w("v", ve)
                p.w("s", ve)
                ln_stats_apply(ft, ft, junkF[:])
                p.w("v", ac)
                p.w("v", rxF.s(ti))
                p.o("v", "tensor_tensor", xv, xv, ft, ALU.add, inc=(ve, 1),
                    cwait=True, cinc=True)
                p.w("y", ve)
                p.o("y", "dma_start", y_out[128 * ti:128 * (ti + 1), :], xv,
                    inc=(rsY.s(ti), 16))
                xr_sl.set(ti, rsY.s(ti))
            p.w("v", ac)

        stF.close()

    for s in sems.values():
        p.w("y", s)

    with nc.Block() as block:
        @block.tensor
        def _(h):
            p.replay("t", h)

        @block.vector
        def _(h):
            p.replay("v", h)

        @block.scalar
        def _(h):
            p.replay("s", h)

        @block.gpsimd
        def _(h):
            p.replay("g", h)

        @block.sync
        def _(h):
            p.replay("y", h)

    return nc, st



def _host_prep(inputs):
    f32 = np.float32
    c = np.ascontiguousarray
    hs = np.asarray(inputs["L0_attn_hidden_states"], dtype=f32)
    common = {
        "wq": c(np.asarray(inputs["L0_attn_w_q"], f32)),
        "wk": c(np.asarray(inputs["L0_attn_w_k"], f32)),
        "wv": c(np.asarray(inputs["L0_attn_w_v"], f32)),
        "wlq": c(np.asarray(inputs["L0_attn_w_lq"], f32)),
        "wlk": c(np.asarray(inputs["L0_attn_w_lk"], f32)),
        "wlv": c(np.asarray(inputs["L0_attn_w_lv"], f32)),
        "wolo": c(np.concatenate([np.asarray(inputs["L0_attn_w_o"], f32),
                                  np.asarray(inputs["L0_attn_w_lo"], f32)], 0)),
        "w1t": c(np.asarray(inputs["L0_ffn_w1q"], f32).T),
        "w3t": c(np.asarray(inputs["L0_ffn_w3q"], f32).T),
        "w2t": c(np.asarray(inputs["L0_ffn_w2q"], f32).T),
        "a13t": c(np.concatenate([np.asarray(inputs["L0_ffn_a1"], f32),
                                  np.asarray(inputs["L0_ffn_a3"], f32)], 0).T),
        "b1t": c(np.asarray(inputs["L0_ffn_b1"], f32).T),
        "b3t": c(np.asarray(inputs["L0_ffn_b3"], f32).T),
        "a2r": _a2_retile(np.asarray(inputs["L0_ffn_a2"], f32)),
        "b2t": c(np.asarray(inputs["L0_ffn_b2"], f32).T),
        "c_eyeb": np.eye(128).astype(ml_dtypes.bfloat16),
        "c_ones": np.ones((128, 129), np.float32).astype(ml_dtypes.bfloat16),
        "c_eyef": np.eye(128).astype(f32),
        "c_mask": np.triu(np.full((128, 128), NEGINF, f32), 1),
        "c_masku": np.triu(np.full((128, 128), NEGINF, f32), 1)
        .astype(ml_dtypes.bfloat16),
        "c_negall": np.full((128, 128), NEGINF, f32).astype(ml_dtypes.bfloat16),
        "c_rope": _rope_tables(),
    }
    return hs, common


def _a2_retile(a2):
    a2t = a2.T
    out = np.zeros((128, 2 * 88), np.float32)
    t = 0
    for j in range(YC):
        R = 128 if j < YC - 1 else 64
        for r in range(4):
            off = YF * r + 128 * j
            out[0:R, 2 * t:2 * t + 2] = a2t[off:off + R]
            t += 1
    return out


def _rope_tables():
    inv = 1.0 / (10000.0 ** (np.arange(0, HD, 2, dtype=np.float64) / HD))
    t = np.arange(T, dtype=np.float64)
    fr = np.outer(t, inv)
    emb = np.concatenate([fr, fr], axis=1)
    cos = np.cos(emb).T.astype(np.float32)
    sin = np.sin(emb).T.astype(np.float32)
    ss = sin.copy()
    ss[0:64] *= -1.0
    tab = np.empty((128, 4 * T), np.float32)
    tab[:, 0 * T:1 * T] = cos * SCALE
    tab[:, 1 * T:2 * T] = ss * SCALE
    tab[:, 2 * T:3 * T] = cos
    tab[:, 3 * T:4 * T] = ss
    return tab.astype(ml_dtypes.bfloat16)


_CACHE = {}


def kernel(**inputs):
    hs, common = _host_prep(inputs)
    nc, st = build_core()
    in_maps = []
    for b in range(8):
        m = dict(common)
        m["x"] = np.ascontiguousarray(hs[b])
        in_maps.append(m)
    res = run_bass_kernel_spmd(nc, in_maps, core_ids=list(range(8)))
    out = np.stack([np.asarray(r["y"]) for r in res.results], axis=0)
    return out.astype(np.float32)


# version 15 (final)
# speedup vs baseline: 1.1858x; 1.1689x over previous
import math
from contextlib import ExitStack

import numpy as np
import ml_dtypes

import concourse.bass as bass
import concourse.mybir as mybir
from concourse.bass_utils import run_bass_kernel_spmd

F32 = mybir.dt.float32
BF16 = mybir.dt.bfloat16
AF = mybir.ActivationFunctionType
ALU = mybir.AluOpType
AX = mybir.AxisListType

T = 2048
D = 4096
HD = 128
NH = 32
NKV = 8
LH = 32
LAT = 1024
HF = 11008
HC = HF // 128
NQC = LAT // 128
YF = HF // 4
YC = 22
DC = D // 128
NT = T // 128
N4 = T // 512
SCALE = 1.0 / math.sqrt(HD)
EPS = 1e-5
NEGINF = -1.0e30

ENGS = ("t", "v", "s", "g", "y")
DEBUG_SLOW = False


class Sem:
    def __init__(self, nc, st, name):
        self.h = st.enter_context(nc.semaphore(name))
        self.t = 0


class Prog:
    def __init__(self):
        self.q = {e: [] for e in ENGS}
        self.chain = {}

    def w(self, e, sem, val=None):
        if val is None:
            val = sem.t
        if val > 0:
            self.q[e].append(("w", sem, val))

    def wg(self, e, guards):
        for (sem, val) in guards:
            self.w(e, sem, val)

    def o(self, e, method, *args, inc=None, cinc=False, cwait=False, **kw):
        if cwait and e in self.chain:
            self.w(e, self.chain[e])
        incs = []
        if inc is not None:
            if isinstance(inc, tuple):
                inc = [inc]
            for (sem, k) in inc:
                sem.t += k
                incs.append((sem, k))
        if cinc and e in self.chain:
            cs = self.chain[e]
            if not any(s is cs for (s, _) in incs):
                assert not incs, "one sem update per instruction"
                cs.t += 1
                incs.append((cs, 1))
        self.q[e].append(("o", method, args, kw, incs))

    def replay(self, e, h):
        for it in self.q[e]:
            if it[0] == "w":
                h.wait_ge(it[1].h, it[2])
            else:
                _, m, args, kw, incs = it
                inst = getattr(h, m)(*args, **kw)
                for (sem, k) in incs:
                    inst = inst.then_inc(sem.h, k)


class Slot:

    def __init__(self, n):
        self.g = [[] for _ in range(n)]
        self.n = n

    def wait(self, p, e, i):
        p.wg(e, self.g[i % self.n])

    def set(self, i, *sems):
        self.g[i % self.n] = [(s, s.t) for s in sems]


def cdiv(a, b):
    return (a + b - 1) // b


class Ring:

    def __init__(self, S, name, n):
        self.sems = [S(name + str(i)) for i in range(n)]
        self.n = n

    def s(self, i):
        return self.sems[i % self.n]


def build_core(dbg=frozenset(), phases="ABCDEF"):
    nc = bass.Bass("TRN2", target_bir_lowering=False)
    st = ExitStack()
    p = Prog()

    def dram(name, shape, dtype, kind=None):
        if kind is None and name in dbg:
            kind = "ExternalOutput"
        if kind is None:
            h = nc.dram_tensor(name, shape, dtype)
        else:
            h = nc.dram_tensor(name, shape, dtype, kind=kind)
        return h.ap()

    x_in = dram("x", [T, D], F32, kind="ExternalInput")
    wq_d = dram("wq", [D, D], F32, kind="ExternalInput")
    wk_d = dram("wk", [D, LAT], F32, kind="ExternalInput")
    wv_d = dram("wv", [D, LAT], F32, kind="ExternalInput")
    wlq_d = dram("wlq", [D, LAT], F32, kind="ExternalInput")
    wlk_d = dram("wlk", [LAT, D], F32, kind="ExternalInput")
    wlv_d = dram("wlv", [D, D], F32, kind="ExternalInput")
    wolo_d = dram("wolo", [2 * D, D], F32, kind="ExternalInput")
    w1t_d = dram("w1t", [LAT, HF], F32, kind="ExternalInput")
    w3t_d = dram("w3t", [LAT, HF], F32, kind="ExternalInput")
    w2t_d = dram("w2t", [YF, D], F32, kind="ExternalInput")
    a13t_d = dram("a13t", [D, 4], F32, kind="ExternalInput")
    b1t_d = dram("b1t", [2, HF], F32, kind="ExternalInput")
    b3t_d = dram("b3t", [2, HF], F32, kind="ExternalInput")
    a2r_d = dram("a2r", [128, 2 * 88], F32, kind="ExternalInput")
    b2t_d = dram("b2t", [2, D], F32, kind="ExternalInput")
    c_eyeb = dram("c_eyeb", [128, 128], BF16, kind="ExternalInput")
    c_ones = dram("c_ones", [128, 129], BF16, kind="ExternalInput")
    c_eyef = dram("c_eyef", [128, 128], F32, kind="ExternalInput")
    c_mask = dram("c_mask", [128, 128], F32, kind="ExternalInput")
    c_masku = dram("c_masku", [128, 128], BF16, kind="ExternalInput")
    c_negall = dram("c_negall", [128, 128], BF16, kind="ExternalInput")
    c_rope = dram("c_rope", [128, 4 * T], BF16, kind="ExternalInput")

    QT = dram("QT", [NH * 128, T], BF16)
    KT = dram("KT", [NKV * 128, T], BF16)
    LKT = dram("LKT", [LH * 128, T], BF16)
    LQT = dram("LQT", [LAT, T], BF16)
    VT = dram("VT", [T, NKV * 128], BF16)
    LVT = dram("LVT", [T, LH * 128], BF16)
    AOT = dram("AOT", [2 * D, T], BF16)
    AOP = dram("AOP", [2, D, T], F32)
    X2 = dram("X2", [T, D], F32)
    LN2T = dram("LN2T", [D, T], BF16)
    GU = dram("GU", [HF, T], BF16)
    y_out = dram("y", [T, D], F32, kind="ExternalOutput")

    sems = {}

    def S(name):
        s = Sem(nc, st, name)
        sems[name] = s
        return s

    dW = S("dW"); dX = S("dX"); dH = S("dH"); dS = S("dS"); dR = S("dR")
    dO = S("dO"); po = S("po"); pe = S("pe"); ve = S("ve"); ac = S("ac")
    cS = S("cS"); cM = S("cM"); cE = S("cE"); cN = S("cN")
    cT = S("cT"); cTeV = S("cTeV"); cTeS = S("cTeS")
    cPV = S("cPV"); cOe = S("cOe")
    p.chain = {"v": ve, "s": ac, "g": po}
    dC = S("dC")
    dC2 = S("dC2")
    dP = S("dP")
    rxA = Ring(S, "rxA", 2)
    rwB = Ring(S, "rwB", 2)
    rsB = Ring(S, "rsB", 2)
    rsV = Ring(S, "rsV", 2)
    rq = Ring(S, "rq", 4)
    rk = Ring(S, "rk", 2)
    rl = Ring(S, "rl", 4)
    rsO = Ring(S, "rsO", 4)
    rwD = Ring(S, "rwD", 2)
    rsA1 = Ring(S, "rsA1", 2)
    rpD = Ring(S, "rpD", 2)
    rxD = Ring(S, "rxD", 2)
    rsX2 = Ring(S, "rsX2", 2)
    rsL = Ring(S, "rsL", 2)
    rlE = Ring(S, "rlE", 2)
    rwE = Ring(S, "rwE", 2)
    rsG = Ring(S, "rsG", 2)
    rgF = Ring(S, "rgF", 3)
    rwF = Ring(S, "rwF", 2)
    rxF = Ring(S, "rxF", 2)
    rsY = Ring(S, "rsY", 2)

    def barrier():
        for e in ENGS:
            for s in sems.values():
                p.w(e, s)

    eyeb = st.enter_context(nc.sbuf_tensor("eyeb", [128, 128], BF16))
    eyef = st.enter_context(nc.sbuf_tensor("eyef", [128, 128], F32))
    maskt = st.enter_context(nc.sbuf_tensor("maskt", [128, 128], F32))
    onesb = st.enter_context(nc.sbuf_tensor("onesb", [128, 129], BF16))
    onesf = st.enter_context(nc.sbuf_tensor("onesf", [128, 1], F32))
    maskub = st.enter_context(nc.sbuf_tensor("maskub", [128, 128], BF16))
    negalb = st.enter_context(nc.sbuf_tensor("negalb", [128, 128], BF16))
    epsv = st.enter_context(nc.sbuf_tensor("epsv", [128, 1], F32))
    stt = st.enter_context(nc.sbuf_tensor("stt", [128, 8], F32))
    sumsq = st.enter_context(nc.sbuf_tensor("sumsq", [128, 1], F32))

    p.o("g", "dma_start", eyeb[:], c_eyeb, inc=(dC, 16))
    p.o("g", "dma_start", eyef[:], c_eyef, inc=(dC, 16))
    p.o("g", "dma_start", maskt[:], c_mask, inc=(dC, 16))
    p.o("g", "dma_start", onesb[:], c_ones, inc=(dC, 16))
    p.o("g", "memset", onesf[:], 1.0, inc=(po, 1))
    p.o("g", "dma_start", maskub[:], c_masku, inc=(dC, 16))
    p.o("g", "dma_start", negalb[:], c_negall, inc=(dC, 16))
    p.o("g", "memset", epsv[:], EPS, inc=(po, 1))

    _lnst_g = [[]]

    def ln_stats_apply(xf32, out_ap, junk, extra_out=None):
        p.w("v", ac)
        p.o("v", "tensor_reduce", stt[:, 0:1], xf32, AX.X, ALU.add, negate=True,
            cwait=True, cinc=True)
        p.o("v", "tensor_scalar", stt[:, 0:1], stt[:, 0:1], 1.0 / D, None,
            ALU.mult, inc=(ve, 1), cwait=True, cinc=True)
        p.wg("s", _lnst_g[0])
        p.o("s", "activation", junk, xf32, AF.Square, accum_out=sumsq[:],
            inc=(ac, 1))
        p.w("v", ac)
        p.o("v", "tensor_tensor", stt[:, 1:2], stt[:, 0:1], stt[:, 0:1], ALU.mult,
            cwait=True, cinc=True)
        p.o("v", "tensor_scalar", stt[:, 2:3], sumsq[:], 1.0 / D, stt[:, 1:2],
            ALU.mult, ALU.subtract, inc=(ve, 1), cwait=True, cinc=True)
        _lnst_g[0] = [(ve, ve.t)]
        p.w("s", ve)
        p.o("s", "activation", stt[:, 3:4], stt[:, 2:3], AF.Sqrt, bias=epsv[:],
            inc=(ac, 1), cwait=True, cinc=True)
        p.w("v", ac)
        p.o("v", "reciprocal", stt[:, 4:5], stt[:, 3:4], cwait=True, cinc=True)
        p.o("v", "tensor_tensor", stt[:, 5:6], stt[:, 0:1], stt[:, 4:5], ALU.mult,
            inc=(ve, 1), cwait=True, cinc=True)
        p.w("s", ve)
        p.o("s", "activation", out_ap, xf32, AF.Identity, bias=stt[:, 5:6],
            scale=stt[:, 4:5], inc=(ac, 1))
        if extra_out is not None:
            p.o("s", "activation", extra_out, xf32, AF.Identity, bias=stt[:, 5:6],
                scale=stt[:, 4:5], inc=(ac, 1))

    stAB = ExitStack()
    LN1T = stAB.enter_context(nc.sbuf_tensor("LN1T", [128, DC * T], BF16))

    if "A" in phases:
        stA = ExitStack()
        xa = stA.enter_context(nc.sbuf_tensor("xa", [128, 2 * D], F32))
        lnt = stA.enter_context(nc.sbuf_tensor("lnt", [128, D], BF16))
        junkA = stA.enter_context(nc.sbuf_tensor("junkA", [128, D], BF16))
        psA = [stA.enter_context(nc.psum_tensor(f"psA{i}", [128, 1024], BF16))
               for i in range(2)]
        xa_sl = Slot(2)
        psA_sl = Slot(2)

        for e in ENGS:
            p.w(e, dC)
            p.w(e, po)

        for i in range(NT):
            xa_sl.wait(p, "y", i)
            xav = xa[:, (i % 2) * D:(i % 2 + 1) * D]
            p.o("y", "dma_start", xav, x_in[128 * i:128 * (i + 1), :],
                inc=(rxA.s(i), 16))
            p.w("v", rxA.s(i))
            p.w("s", rxA.s(i))
            ln_stats_apply(xav, lnt[:], junkA[:])
            xa_sl.set(i, ac, ve)
            p.w("t", ac)
            for g in range(4):
                par = g % 2
                psA_sl.wait(p, "t", g + 4 * i)
                for j in range(8):
                    c = 8 * g + j
                    p.o("t", "transpose", psA[par][:, 128 * j:128 * (j + 1)],
                        lnt[:, 128 * c:128 * (c + 1)], eyeb[:],
                        inc=(pe, 1) if j == 7 else None)
                p.w("v", pe)
                seg = LN1T[:, 8 * g * T:8 * (g + 1) * T]
                seg3 = seg.rearrange("p (c t) -> p c t", c=8)
                p.o("v", "tensor_copy", seg3[:, :, 128 * i:128 * (i + 1)],
                    psA[par].rearrange("p (c t) -> p c t", c=8), inc=(ve, 1))
                psA_sl.set(g + 4 * i, ve)
            if DEBUG_SLOW:
                barrier()
        barrier()
        if "ln1t_dbg" in dbg and "B" not in phases:
            ln1t_dbg = dram("ln1t_dbg", [D, T], BF16, kind="ExternalOutput")
            for c in range(DC):
                p.o("y", "dma_start", ln1t_dbg[128 * c:128 * (c + 1), :],
                    LN1T[:, T * c:T * (c + 1)], inc=(dS, 16))
            barrier()
        stA.close()

    if "B" in phases:
        stB = ExitStack()
        bufs = {}

        def alloc_b(stk, with_vt):
            bufs["ropes"] = stk.enter_context(
                nc.sbuf_tensor("ropes" + str(with_vt), [128, 4 * T], BF16))
            bufs["wfm"] = stk.enter_context(
                nc.sbuf_tensor("wfm" + str(with_vt), [128, 2 * DC * 128], BF16))
            bufs["hbr"] = stk.enter_context(
                nc.sbuf_tensor("hbr" + str(with_vt), [128, T], BF16))
            bufs["hbs"] = stk.enter_context(
                nc.sbuf_tensor("hbs" + str(with_vt), [128, T], BF16))
            bufs["hbo"] = stk.enter_context(
                nc.sbuf_tensor("hbo" + str(with_vt), [128, 2 * T], BF16))
            bufs["psB"] = [stk.enter_context(
                nc.psum_tensor(f"psB{with_vt}{i}", [128, 512], F32))
                for i in range(2)]
            if with_vt:
                bufs["vtt"] = stk.enter_context(
                    nc.sbuf_tensor("vtt", [128, 2 * 512], BF16))
                bufs["psBT"] = [stk.enter_context(
                    nc.psum_tensor(f"psBT{i}", [128, 512], BF16))
                    for i in range(2)]
            p.o("g", "dma_start", bufs["ropes"][:], c_rope, inc=(dC2, 16))

        alloc_b(stB, 1)
        wfm_sl = Slot(2)
        psB_sl = Slot(2)
        hbr_sl = Slot(1)
        hbs_sl = Slot(1)
        hbo_sl = Slot(2)
        vtt_sl = Slot(2)
        psBT_sl = Slot(2)

        def emit_fm_chunk(wsrc, m, kc, mmi, rhs_tile):
            wfm, hbr, psB = bufs["wfm"], bufs["hbr"], bufs["psB"]
            par = mmi % 2
            wfm_sl.wait(p, "g", mmi)
            wsl = wsrc[:, 128 * m:128 * (m + 1)]
            wdst = wfm[:, par * DC * 128:par * DC * 128 + kc * 128]
            p.o("g", "dma_start", wdst.rearrange("p (k m) -> p k m", k=kc),
                wsl.rearrange("(k p) m -> p k m", p=128), inc=(rwB.s(mmi), 16))
            p.w("t", rwB.s(mmi))
            p.w("t", dC2)
            hbr_sl.wait(p, "s", 0)
            for n in range(N4):
                psB_sl.wait(p, "t", n)
                for k in range(kc):
                    p.o("t", "matmul", psB[n % 2][:],
                        wfm[:, par * DC * 128 + 128 * k:
                            par * DC * 128 + 128 * (k + 1)],
                        rhs_tile(k, n),
                        start=(k == 0), stop=(k == kc - 1),
                        inc=(pe, 1) if k == kc - 1 else None)
                p.w("s", pe)
                p.o("s", "copy", hbr[:, 512 * n:512 * (n + 1)], psB[n % 2][:],
                    inc=(ac, 1))
                psB_sl.set(n, ac)
            wfm_sl.set(mmi, pe)

        def emit_rope_store(m, ci, dest, mmi):
            hbr, hbs, hbo, ropes = (bufs["hbr"], bufs["hbs"], bufs["hbo"],
                                    bufs["ropes"])
            hbs_sl.wait(p, "y", 0)
            p.w("y", ac)
            p.o("y", "dma_start", hbs[0:64, :], hbr[64:128, :], inc=(dR, 16))
            p.o("y", "dma_start", hbs[64:128, :], hbr[0:64, :], inc=(dR, 16))
            p.w("v", dR)
            p.w("v", ac)
            p.w("v", dC2)
            hbo_sl.wait(p, "v", mmi)
            ob = hbo[:, (mmi % 2) * T:(mmi % 2 + 1) * T]
            p.o("v", "tensor_tensor", ob, hbr[:], ropes[:, ci * T:(ci + 1) * T],
                ALU.mult, cwait=True, cinc=True)
            p.o("v", "tensor_tensor", hbs[:], hbs[:],
                ropes[:, (ci + 1) * T:(ci + 2) * T], ALU.mult, cinc=True)
            p.o("v", "tensor_tensor", ob, ob, hbs[:], ALU.add, inc=(ve, 1),
                cwait=True, cinc=True)
            hbr_sl.set(0, ve)
            hbs_sl.set(0, ve)
            p.w("y", ve)
            p.o("y", "dma_start", dest[128 * m:128 * (m + 1), :], ob,
                inc=(rsB.s(mmi), 16))
            hbo_sl.set(mmi, rsB.s(mmi))

        def emit_plain_store(m, dest):
            p.w("y", ac)
            p.o("y", "dma_start", dest[128 * m:128 * (m + 1), :], bufs["hbr"][:],
                inc=(dP, 16))
            hbr_sl.set(0, dP)

        def emit_tm_store(m, dest, mmi):
            hbr, vtt, psBT = bufs["hbr"], bufs["vtt"], bufs["psBT"]
            for n in range(N4):
                tpar = (mmi * N4 + n) % 2
                p.w("t", ac)
                psBT_sl.wait(p, "t", mmi * N4 + n)
                for j in range(4):
                    p.o("t", "transpose", psBT[tpar][:, 128 * j:128 * (j + 1)],
                        hbr[:, 512 * n + 128 * j:512 * n + 128 * (j + 1)],
                        eyeb[:], inc=(pe, 1) if j == 3 else None)
                p.w("v", pe)
                vtt_sl.wait(p, "v", mmi * N4 + n)
                vtv = vtt[:, tpar * 512:(tpar + 1) * 512]
                p.o("v", "tensor_copy", vtv, psBT[tpar][:], inc=(ve, 1))
                psBT_sl.set(mmi * N4 + n, ve)
                p.w("y", ve)
                dseg = dest[512 * n:512 * (n + 1), 128 * m:128 * (m + 1)]
                p.o("y", "dma_start", dseg.rearrange("(j p) f -> p j f", p=128),
                    vtv.rearrange("p (j f) -> p j f", j=4),
                    inc=(rsV.s(mmi * N4 + n), 16))
                vtt_sl.set(mmi * N4 + n, rsV.s(mmi * N4 + n))
            hbr_sl.set(0, pe)

        def ln1_rhs(k, n):
            return LN1T[:, T * k + 512 * n:T * k + 512 * (n + 1)]

        mmi = 0
        for (wsrc, MCH, kind, dest) in [
                (wq_d, NH, "rope_q", QT),
                (wk_d, NKV, "rope_k", KT),
                (wlq_d, NQC, "plain", LQT),
                (wv_d, NKV, "tm", VT),
                (wlv_d, LH, "tm", LVT)]:
            for m in range(MCH):
                emit_fm_chunk(wsrc, m, DC, mmi, ln1_rhs)
                if kind == "rope_q":
                    emit_rope_store(m, 0, dest, mmi)
                elif kind == "rope_k":
                    emit_rope_store(m, 2, dest, mmi)
                elif kind == "plain":
                    emit_plain_store(m, dest)
                else:
                    emit_tm_store(m, dest, mmi)
                mmi += 1

        barrier()
        if "ln1t_dbg" in dbg:
            ln1t_dbg = dram("ln1t_dbg", [D, T], BF16, kind="ExternalOutput")
            for c in range(DC):
                p.o("y", "dma_start", ln1t_dbg[128 * c:128 * (c + 1), :],
                    LN1T[:, T * c:T * (c + 1)], inc=(dS, 16))
            barrier()
        stB.close()
        stAB.close()
        stB = ExitStack()
        alloc_b(stB, 0)
        wfm_sl = Slot(2)
        psB_sl = Slot(2)
        hbr_sl = Slot(1)
        hbs_sl = Slot(1)
        hbo_sl = Slot(2)
        lqs = stB.enter_context(nc.sbuf_tensor("lqs", [128, NQC * T], BF16))
        for k in range(NQC):
            p.o("y", "dma_start", lqs[:, T * k:T * (k + 1)],
                LQT[128 * k:128 * (k + 1), :], inc=(dX, 16))
        p.w("t", dX)

        def lq_rhs(k, n):
            return lqs[:, T * k + 512 * n:T * k + 512 * (n + 1)]

        for m in range(LH):
            emit_fm_chunk(wlk_d, m, NQC, mmi, lq_rhs)
            emit_rope_store(m, 2, LKT, mmi)
            mmi += 1

        barrier()
        stB.close()
    else:
        stAB.close()

    if "C" in phases:
        stC = ExitStack()
        qh = stC.enter_context(nc.sbuf_tensor("qh", [128, 4 * T], BF16))
        kh = stC.enter_context(nc.sbuf_tensor("kh", [128, 2 * T], BF16))
        lkh = stC.enter_context(nc.sbuf_tensor("lkh", [128, 4 * T], BF16))
        vh = stC.enter_context(nc.sbuf_tensor("vh", [128, 2 * T], BF16))
        lvh = stC.enter_context(nc.sbuf_tensor("lvh", [128, 4 * T], BF16))
        PT = stC.enter_context(nc.sbuf_tensor("PT", [128, 2 * 1024], BF16))
        ohb = stC.enter_context(nc.sbuf_tensor("ohb", [128, 4 * T], BF16))
        sumall = stC.enter_context(nc.sbuf_tensor("sumall", [1, 4 * T], F32))
        recb = stC.enter_context(nc.sbuf_tensor("recb", [1, T], BF16))
        SA = [stC.enter_context(nc.psum_tensor(f"SA{i}", [128, 1024], F32))
              for i in range(2)]
        OA = [stC.enter_context(nc.psum_tensor(f"OA{i}", [128, 512], F32))
              for i in range(2)]
        SU = [stC.enter_context(nc.psum_tensor(f"SU{i}", [1, 512], F32))
              for i in range(2)]
        SA_sl = Slot(2)
        PT_sl = Slot(2)
        OA_sl = Slot(2)
        SU_sl = Slot(2)
        RB_sl = Slot(1)
        oh_sl = Slot(4)
        pair_done = {}

        def attn_quad(qvs, kbs, vbs, rows):
            sstep = 0
            ce_hist = {}
            for m in range(N4):
                for u in range(4):
                    hh = u // 2
                    qv = qvs[hh]
                    ktiles = 4 * m + 4
                    kpairs = ktiles // 2
                    useq = 4 * m + u
                    oa = OA[u % 2]
                    su = SU[u % 2]
                    OA_sl.wait(p, "t", useq)
                    SU_sl.wait(p, "t", useq)
                    for t in range(kpairs + 1):
                        if t < kpairs:
                            sa = SA[sstep % 2]
                            SA_sl.wait(p, "t", sstep)
                            for half in range(2):
                                kt = 2 * t + half
                                o = kt - 4 * m
                                sav = sa[:, 512 * half:512 * (half + 1)]
                                lastmm = (half == 1) and (o < 0)
                                p.o("t", "matmul", sav,
                                    kbs[u][:, 128 * kt:128 * (kt + 1)],
                                    qv[:, 512 * m:512 * (m + 1)],
                                    start=True, stop=(o < 0),
                                    inc=(cS, 1) if lastmm else None)
                                if o >= 0:
                                    for j in range(o + 1):
                                        lt = negalb if j < o else maskub
                                        p.o("t", "matmul",
                                            sav[:, 128 * j:128 * (j + 1)],
                                            lt[:], eyeb[:], start=False,
                                            stop=(j == o),
                                            inc=(cS, 1) if (j == o and half == 1)
                                            else None)
                            p.w("s", cS)
                            PT_sl.wait(p, "s", sstep)
                            ptv = PT[:, (sstep % 2) * 1024:(sstep % 2 + 1) * 1024]
                            p.o("s", "activation", ptv, sa[:], AF.Exp,
                                inc=(cE, 1))
                            ce_hist[sstep] = cE.t
                            SA_sl.set(sstep, cE)
                        if t >= 1:
                            pstep = sstep - 1
                            p.w("t", cE, ce_hist[pstep])
                            for half in range(2):
                                kt = 2 * (t - 1) + half
                                pv_ = PT[:, (pstep % 2) * 1024 + 512 * half:
                                         (pstep % 2) * 1024 + 512 * (half + 1)]
                                p.o("t", "matmul", oa[:],
                                    vbs[u][:, 128 * kt:128 * (kt + 1)], pv_,
                                    start=(kt == 0), stop=(kt == ktiles - 1),
                                    inc=(cPV, 1) if kt == ktiles - 1 else None)
                                p.o("t", "matmul", su[0:1, :], onesb[:, 0:1],
                                    pv_,
                                    start=(kt == 0), stop=(kt == ktiles - 1),
                                    inc=(pe, 1) if half == 1 else None)
                            PT_sl.set(pstep, pe)
                        if t < kpairs:
                            sstep += 1
                    p.w("v", cPV)
                    if m == 0:
                        oh_sl.wait(p, "v", u)
                    p.o("v", "tensor_copy",
                        ohb[:, u * T + 512 * m:u * T + 512 * (m + 1)],
                        oa[:], inc=(cOe, 1))
                    OA_sl.set(useq, cOe)
                    p.w("v", pe)
                    p.o("v", "tensor_copy",
                        sumall[0:1, u * T + 512 * m:u * T + 512 * (m + 1)],
                        su[0:1, :], inc=(cTeV, 1))
                    SU_sl.set(useq, cTeV)
            for u in range(4):
                p.o("v", "reciprocal", sumall[0:1, u * T:(u + 1) * T],
                    sumall[0:1, u * T:(u + 1) * T], cwait=True, cinc=True)
                p.w("s", ve)
                p.o("s", "copy", recb[0:1, :], sumall[0:1, u * T:(u + 1) * T],
                    inc=(ac, 1))
                for c in range(N4):
                    RB_sl.wait(p, "t", 0)
                    p.w("t", ac)
                    p.w("t", cE)
                    rbv = SA[0][:, 0:512]
                    p.o("t", "matmul", rbv, onesb[0:1, 0:128],
                        recb[0:1, 512 * c:512 * (c + 1)],
                        start=True, stop=True, inc=(cT, 1))
                    p.w("v", cT)
                    p.w("v", cOe)
                    p.o("v", "tensor_tensor",
                        ohb[:, u * T + 512 * c:u * T + 512 * (c + 1)],
                        ohb[:, u * T + 512 * c:u * T + 512 * (c + 1)],
                        rbv, ALU.mult, inc=(ve, 1), cwait=True)
                    RB_sl.set(0, ve)
                p.w("y", ve)
                p.o("y", "dma_start",
                    AOT[128 * rows[u]:(128 * (rows[u] + 1)), :],
                    ohb[:, u * T:(u + 1) * T], inc=(rsO.s(u), 16))
                oh_sl.set(u, rsO.s(u))
            SA_sl.set(0, ve)

        for h0 in range(0, NH, 2):
            kvi = h0 // 4
            pp = (h0 // 2) % 2
            guards = pair_done.get(h0 - 4, [])
            p.wg("y", guards)
            for hh in (0, 1):
                h = h0 + hh
                b = 2 * pp + hh
                p.o("y", "dma_start", qh[:, b * T:(b + 1) * T],
                    QT[128 * h:128 * (h + 1), :], inc=(rq.s(b), 16))
                p.o("y", "dma_start", lkh[:, b * T:(b + 1) * T],
                    LKT[128 * h:128 * (h + 1), :], inc=(rl.s(b), 16))
                lvseg = LVT[:, 128 * h:128 * (h + 1)]
                lvv = lvh[:, b * T:(b + 1) * T]
                p.o("y", "dma_start", lvv.rearrange("p (t c) -> p t c", t=NT),
                    lvseg.rearrange("(t p) c -> p t c", p=128),
                    inc=(rl.s(b), 16))
            do_k0 = (h0 % 4 == 0)
            if do_k0:
                p.wg("y", pair_done.get(h0 - 8, []))
                p.o("y", "dma_start", kh[:, (kvi % 2) * T:(kvi % 2 + 1) * T],
                    KT[128 * kvi:128 * (kvi + 1), :], inc=(rk.s(kvi), 16))
                vseg = VT[:, 128 * kvi:128 * (kvi + 1)]
                vhv = vh[:, (kvi % 2) * T:(kvi % 2 + 1) * T]
                p.o("y", "dma_start", vhv.rearrange("p (t c) -> p t c", t=NT),
                    vseg.rearrange("(t p) c -> p t c", p=128),
                    inc=(rk.s(kvi), 16))
            for e in ("t", "s"):
                p.w(e, rq.s(2 * pp)); p.w(e, rq.s(2 * pp + 1))
                p.w(e, rk.s(kvi))
                p.w(e, rl.s(2 * pp)); p.w(e, rl.s(2 * pp + 1))
            kvv = kh[:, (kvi % 2) * T:(kvi % 2 + 1) * T]
            vvv = vh[:, (kvi % 2) * T:(kvi % 2 + 1) * T]
            qvs = (qh[:, 2 * pp * T:(2 * pp + 1) * T],
                   qh[:, (2 * pp + 1) * T:(2 * pp + 2) * T])
            kbs = (kvv, lkh[:, 2 * pp * T:(2 * pp + 1) * T], kvv,
                   lkh[:, (2 * pp + 1) * T:(2 * pp + 2) * T])
            vbs = (vvv, lvh[:, 2 * pp * T:(2 * pp + 1) * T], vvv,
                   lvh[:, (2 * pp + 1) * T:(2 * pp + 2) * T])
            rows = (h0, NH + h0, h0 + 1, NH + h0 + 1)
            attn_quad(qvs, kbs, vbs, rows)
            pair_done[h0] = [(cS, cS.t), (cPV, cPV.t), (ac, ac.t),
                             (ve, ve.t), (pe, pe.t)]

        barrier()
        stC.close()

    if "D" in phases:
        stD1 = ExitStack()
        rhsD = stD1.enter_context(nc.sbuf_tensor("rhsD", [128, 32 * T], BF16))
        wodb = stD1.enter_context(nc.sbuf_tensor("wodb", [128, 2 * 32 * 128], BF16))
        aob = stD1.enter_context(nc.sbuf_tensor("aob", [128, 2 * 512], F32))
        psD1 = [stD1.enter_context(nc.psum_tensor(f"psD1{i}", [128, 512], F32))
                for i in range(2)]
        wod_sl = Slot(2)
        aob_sl = Slot(2)
        psD_sl = Slot(2)

        for kg in range(2):
            p.w("y", pe)
            for r in range(8):
                seg = AOT[4096 * kg + 512 * r:4096 * kg + 512 * (r + 1), :]
                p.o("y", "dma_start",
                    rhsD[:, 4 * r * T:4 * (r + 1) * T].rearrange(
                        "p (k t) -> p k t", k=4),
                    seg.rearrange("(k p) t -> p k t", p=128), inc=(dX, 16))
            for m in range(DC):
                par = m % 2
                wod_sl.wait(p, "g", m)
                wsl = wolo_d[4096 * kg:4096 * (kg + 1), 128 * m:128 * (m + 1)]
                wdst = wodb[:, par * 32 * 128:(par + 1) * 32 * 128]
                p.o("g", "dma_start", wdst.rearrange("p (k m) -> p k m", k=32),
                    wsl.rearrange("(k p) m -> p k m", p=128),
                    inc=(rwD.s(m), 16))
                p.w("t", rwD.s(m))
                if m == 0:
                    p.w("t", dX)
                for n in range(N4):
                    psD_sl.wait(p, "t", n)
                    for k in range(32):
                        p.o("t", "matmul", psD1[n % 2][:],
                            wodb[:, par * 32 * 128 + 128 * k:
                                 par * 32 * 128 + 128 * (k + 1)],
                            rhsD[:, k * T + 512 * n:k * T + 512 * (n + 1)],
                            start=(k == 0), stop=(k == 31),
                            inc=(pe, 1) if k == 31 else None)
                    p.w("s", pe)
                    aob_sl.wait(p, "s", n)
                    av = aob[:, (n % 2) * 512:(n % 2 + 1) * 512]
                    p.o("s", "copy", av, psD1[n % 2][:], inc=(ac, 1))
                    psD_sl.set(n, ac)
                    p.w("y", ac)
                    p.o("y", "dma_start",
                        AOP[kg, 128 * m:128 * (m + 1), 512 * n:512 * (n + 1)],
                        av, inc=(rsA1.s(n), 16))
                    aob_sl.set(n, rsA1.s(n))
                wod_sl.set(m, pe)
        barrier()
        stD1.close()

    stDE = ExitStack()
    XFT = stDE.enter_context(nc.sbuf_tensor("XFT", [128, NQC * T], BF16))

    if "D" in phases:
        stD2 = ExitStack()
        pbuf = stD2.enter_context(nc.sbuf_tensor("pbuf", [128, 2 * D], F32))
        accD = stD2.enter_context(nc.sbuf_tensor("accD", [128, D], F32))
        atok = stD2.enter_context(nc.sbuf_tensor("atok", [128, D], F32))
        xrD = stD2.enter_context(nc.sbuf_tensor("xrD", [128, D], F32))
        ln2b = stD2.enter_context(nc.sbuf_tensor("ln2b", [128, D], BF16))
        xffold = stD2.enter_context(nc.sbuf_tensor("xffold", [128, LAT], F32))
        xfb = stD2.enter_context(nc.sbuf_tensor("xfb", [128, LAT], BF16))
        junkD = stD2.enter_context(nc.sbuf_tensor("junkD", [128, D], BF16))
        l2st = stD2.enter_context(nc.sbuf_tensor("l2st", [128, 2 * 1024], BF16))
        psDT = [stD2.enter_context(nc.psum_tensor(f"psDT{i}", [128, 512], F32))
                for i in range(2)]
        psDB = [stD2.enter_context(nc.psum_tensor(f"psDB{i}", [128, 1024], BF16))
                for i in range(2)]
        pb_sl = Slot(2)
        psDT_sl = Slot(2)
        psDB_sl = Slot(2)
        l2st_sl = Slot(2)
        xrD_sl = Slot(1)

        for ti in range(NT):
            for kg in range(2):
                pb_sl.wait(p, "y", 2 * ti + kg)
                pv = pbuf[:, (kg % 2) * D:(kg % 2 + 1) * D]
                seg = AOP[kg, :, 128 * ti:128 * (ti + 1)]
                p.o("y", "dma_start", pv.rearrange("p (c t) -> p c t", c=DC),
                    seg.rearrange("(c p) t -> p c t", p=128),
                    inc=(rpD.s(2 * ti + kg), 16))
            p.w("v", rpD.s(2 * ti))
            p.w("v", rpD.s(2 * ti + 1))
            p.w("v", ac)
            p.w("v", pe)
            p.o("v", "tensor_tensor", accD[:],
                pbuf[:, 0:D], pbuf[:, D:2 * D], ALU.add,
                inc=(ve, 1), cwait=True, cinc=True)
            pb_sl.set(2 * ti, ve)
            pb_sl.set(2 * ti + 1, ve)
            xrD_sl.wait(p, "y", ti)
            p.o("y", "dma_start", xrD[:], x_in[128 * ti:128 * (ti + 1), :],
                inc=(rxD.s(ti), 16))
            p.w("t", ve)
            for g in range(8):
                par = g % 2
                psDT_sl.wait(p, "t", g)
                for j in range(4):
                    c = 4 * g + j
                    p.o("t", "transpose", psDT[par][:, 128 * j:128 * (j + 1)],
                        accD[:, 128 * c:128 * (c + 1)], eyef[:],
                        inc=(pe, 1) if j == 3 else None)
                p.w("v", pe)
                p.o("v", "tensor_copy", atok[:, 512 * g:512 * (g + 1)],
                    psDT[par][:], inc=(ve, 1))
                psDT_sl.set(g, ve)
            p.w("s", ve)
            p.w("v", ve)
            ln_stats_apply(atok[:], atok[:], junkD[:])
            p.w("v", ac)
            p.w("v", rxD.s(ti))
            p.o("v", "tensor_tensor", xrD[:], xrD[:], atok[:], ALU.add,
                inc=(ve, 1), cwait=True, cinc=True)
            p.w("y", ve)
            p.o("y", "dma_start", X2[128 * ti:128 * (ti + 1), :], xrD[:],
                inc=(rsX2.s(ti), 16))
            p.w("s", ve)
            ln_stats_apply(xrD[:], atok[:], junkD[:])
            xrD_sl.set(ti, ac, rsX2.s(ti))
            p.w("v", ac)
            p.o("v", "tensor_copy", ln2b[:], atok[:], cwait=True, cinc=True)
            p.w("v", ac)
            p.o("v", "tensor_tensor", xffold[:], atok[:, 0:LAT],
                atok[:, LAT:2 * LAT], ALU.add, cwait=True, cinc=True)
            p.o("v", "tensor_tensor", xffold[:], xffold[:],
                atok[:, 2 * LAT:3 * LAT], ALU.add, cwait=True, cinc=True)
            p.o("v", "tensor_tensor", xffold[:], xffold[:],
                atok[:, 3 * LAT:4 * LAT], ALU.add, cwait=True, cinc=True)
            p.o("v", "tensor_copy", xfb[:], xffold[:], inc=(ve, 1), cwait=True)
            p.w("t", ac)
            p.w("t", ve)
            for g in range(4):
                par = g % 2
                psDB_sl.wait(p, "t", g)
                for j in range(8):
                    c = 8 * g + j
                    p.o("t", "transpose", psDB[par][:, 128 * j:128 * (j + 1)],
                        ln2b[:, 128 * c:128 * (c + 1)], eyeb[:],
                        inc=(pe, 1) if j == 7 else None)
                p.w("v", pe)
                l2st_sl.wait(p, "v", 4 * ti + g)
                lsv = l2st[:, (g % 2) * 1024:(g % 2 + 1) * 1024]
                p.o("v", "tensor_copy", lsv, psDB[par][:], inc=(ve, 1))
                psDB_sl.set(g, ve)
                p.w("y", ve)
                seg = LN2T[128 * 8 * g:128 * 8 * (g + 1),
                           128 * ti:128 * (ti + 1)]
                p.o("y", "dma_start", seg.rearrange("(c p) t -> p c t", p=128),
                    lsv.rearrange("p (c t) -> p c t", c=8),
                    inc=(rsL.s(4 * ti + g), 16))
                l2st_sl.set(4 * ti + g, rsL.s(4 * ti + g))
            psDB_sl.wait(p, "t", 4)
            for j in range(8):
                p.o("t", "transpose", psDB[0][:, 128 * j:128 * (j + 1)],
                    xfb[:, 128 * j:128 * (j + 1)], eyeb[:],
                    inc=(pe, 1) if j == 7 else None)
            p.w("v", pe)
            segx = XFT.rearrange("p (c t) -> p c t", c=NQC)
            p.o("v", "tensor_copy", segx[:, :, 128 * ti:128 * (ti + 1)],
                psDB[0].rearrange("p (c t) -> p c t", c=8), inc=(ve, 1))
            psDB_sl.set(4, ve)

        barrier()
        stD2.close()

    if "E" in phases:
        stE = ExitStack()
        a13b = stE.enter_context(nc.sbuf_tensor("a13b", [128, DC * 4], BF16))
        b1b = stE.enter_context(nc.sbuf_tensor("b1b", [2, HF], BF16))
        b3b = stE.enter_context(nc.sbuf_tensor("b3b", [2, HF], BF16))
        lnsE = stE.enter_context(nc.sbuf_tensor("lnsE", [128, 2 * 512], BF16))
        xgf = stE.enter_context(nc.sbuf_tensor("xgf", [2, T], F32))
        xuf = stE.enter_context(nc.sbuf_tensor("xuf", [2, T], F32))
        xgb = stE.enter_context(nc.sbuf_tensor("xgb", [2, T], BF16))
        xub = stE.enter_context(nc.sbuf_tensor("xub", [2, T], BF16))
        w13 = stE.enter_context(nc.sbuf_tensor("w13", [128, 2 * 16 * 128], BF16))
        sgb = stE.enter_context(nc.sbuf_tensor("sgb", [128, 2 * 512], BF16))
        upb = stE.enter_context(nc.sbuf_tensor("upb", [128, 2 * 512], BF16))
        gur = stE.enter_context(nc.sbuf_tensor("gur", [128, 2 * T], BF16))
        psE = [stE.enter_context(nc.psum_tensor(f"psE{i}", [128, 512], F32))
               for i in range(4)]
        psEA1 = stE.enter_context(nc.psum_tensor("psEA1", [2, 512], F32))
        psEA2 = stE.enter_context(nc.psum_tensor("psEA2", [2, 512], F32))
        lnsE_sl = Slot(2)
        w13_sl = Slot(2)
        gur_sl = Slot(2)

        p.o("g", "dma_start", a13b[:].rearrange("p (c r) -> p c r", c=DC),
            a13t_d.rearrange("(c p) r -> p c r", p=128), inc=(dC2, 16))
        p.o("g", "dma_start", b1b[:], b1t_d, inc=(dC2, 16))
        p.o("g", "dma_start", b3b[:], b3t_d, inc=(dC2, 16))

        p.w("t", dC2)
        for n in range(N4):
            p.w("t", ve)
            for k in range(DC):
                ci = n * DC + k
                lnsE_sl.wait(p, "y", ci)
                lv = lnsE[:, (ci % 2) * 512:(ci % 2 + 1) * 512]
                p.o("y", "dma_start", lv,
                    LN2T[128 * k:128 * (k + 1), 512 * n:512 * (n + 1)],
                    inc=(rlE.s(ci), 16))
                p.w("t", rlE.s(ci))
                p.o("t", "matmul", psEA1[0:2, :], a13b[:, 4 * k:4 * k + 2],
                    lv, start=(k == 0), stop=(k == DC - 1),
                    inc=None)
                p.o("t", "matmul", psEA2[0:2, :], a13b[:, 4 * k + 2:4 * k + 4],
                    lv, start=(k == 0), stop=(k == DC - 1),
                    inc=(pe, 1))
                lnsE_sl.set(ci, pe)
            p.w("v", pe)
            p.o("v", "tensor_copy", xgf[:, 512 * n:512 * (n + 1)],
                psEA1[0:2, :])
            p.o("v", "tensor_copy", xuf[:, 512 * n:512 * (n + 1)],
                psEA2[0:2, :], inc=(ve, 1))
        p.w("s", ve)
        p.o("s", "mul", xgb[:], xgf[:], 0.5)
        p.o("s", "mul", xub[:], xuf[:], 0.5, inc=(ac, 1))
        xa13_guard = [(ac, ac.t)]
        psE_sl = Slot(2)
        sg_sl = Slot(2)

        for m in range(HC):
            par = m % 2
            w13_sl.wait(p, "g", m)
            wdst1 = w13[:, par * 16 * 128:par * 16 * 128 + NQC * 128]
            wdst3 = w13[:, par * 16 * 128 + NQC * 128:(par + 1) * 16 * 128]
            p.o("g", "dma_start", wdst1.rearrange("p (k m) -> p k m", k=NQC),
                w1t_d[:, 128 * m:128 * (m + 1)].rearrange("(k p) m -> p k m", p=128),
                inc=(rwE.s(m), 16))
            p.o("g", "dma_start", wdst3.rearrange("p (k m) -> p k m", k=NQC),
                w3t_d[:, 128 * m:128 * (m + 1)].rearrange("(k p) m -> p k m", p=128),
                inc=(rwE.s(m), 16))
            p.w("t", rwE.s(m))
            p.wg("t", xa13_guard)
            for n in range(N4):
                gp = psE[2 * (n % 2)]
                up = psE[2 * (n % 2) + 1]
                psE_sl.wait(p, "t", n)
                for k in range(NQC):
                    p.o("t", "matmul", gp[:],
                        w13[:, par * 16 * 128 + 128 * k:
                            par * 16 * 128 + 128 * (k + 1)],
                        XFT[:, T * k + 512 * n:T * k + 512 * (n + 1)],
                        start=(k == 0), stop=False)
                p.o("t", "matmul", gp[:], b1b[0:2, 128 * m:128 * (m + 1)],
                    xgb[0:2, 512 * n:512 * (n + 1)], start=False, stop=True,
                    inc=(pe, 1))
                for k in range(NQC):
                    p.o("t", "matmul", up[:],
                        w13[:, par * 16 * 128 + NQC * 128 + 128 * k:
                            par * 16 * 128 + NQC * 128 + 128 * (k + 1)],
                        XFT[:, T * k + 512 * n:T * k + 512 * (n + 1)],
                        start=(k == 0), stop=False)
                p.o("t", "matmul", up[:], b3b[0:2, 128 * m:128 * (m + 1)],
                    xub[0:2, 512 * n:512 * (n + 1)], start=False, stop=True,
                    inc=(pe, 1))
                p.w("s", pe)
                sg_sl.wait(p, "s", n)
                sgv = sgb[:, (n % 2) * 512:(n % 2 + 1) * 512]
                upv = upb[:, (n % 2) * 512:(n % 2 + 1) * 512]
                p.o("s", "activation", sgv, gp[:], AF.Silu)
                p.o("s", "copy", upv, up[:], inc=(ac, 1))
                psE_sl.set(n, ac)
                p.w("v", ac)
                gur_sl.wait(p, "v", m)
                gv = gur[:, (m % 2) * T + 512 * n:(m % 2) * T + 512 * (n + 1)]
                p.o("v", "tensor_tensor", gv, sgv, upv, ALU.mult,
                    inc=(ve, 1))
                sg_sl.set(n, ve)
            w13_sl.set(m, pe)
            p.w("y", ve)
            p.o("y", "dma_start", GU[128 * m:128 * (m + 1), :],
                gur[:, (m % 2) * T:(m % 2 + 1) * T], inc=(rsG.s(m), 16))
            gur_sl.set(m, rsG.s(m))

        barrier()
        stE.close()
    stDE.close()

    if "F" in phases:
        stF = ExitStack()
        yfa = stF.enter_context(nc.sbuf_tensor("yfa", [128, YC * 512], F32))
        yfb = stF.enter_context(nc.sbuf_tensor("yfb", [128, YC * 512], BF16))
        gub = stF.enter_context(nc.sbuf_tensor("gub", [128, 3 * 512], BF16))
        wd2 = stF.enter_context(nc.sbuf_tensor("wd2", [128, 2 * YC * 128], BF16))
        a2sb = stF.enter_context(nc.sbuf_tensor("a2sb", [128, 2 * 88], BF16))
        b2sb = stF.enter_context(nc.sbuf_tensor("b2sb", [2, D], BF16))
        xa2f = stF.enter_context(nc.sbuf_tensor("xa2f", [2, 512], F32))
        xa2b = stF.enter_context(nc.sbuf_tensor("xa2b", [2, 512], BF16))
        fof = stF.enter_context(nc.sbuf_tensor("fof", [128, 2 * 512], F32))
        ftok = stF.enter_context(nc.sbuf_tensor("ftok", [128, 4 * D], F32))
        xrF = stF.enter_context(nc.sbuf_tensor("xrF", [128, 2 * D], F32))
        junkF = stF.enter_context(nc.sbuf_tensor("junkF", [128, D], BF16))
        psF = [stF.enter_context(nc.psum_tensor(f"psF{i}", [128, 512], F32))
               for i in range(2)]
        psFT = [stF.enter_context(nc.psum_tensor(f"psFT{i}", [128, 512], F32))
                for i in range(2)]
        psFA = stF.enter_context(nc.psum_tensor("psFA", [2, 512], F32))
        gub_sl = Slot(3)
        wd2_sl = Slot(2)
        psF_sl = Slot(2)
        fof_sl = Slot(2)
        psFT_sl = Slot(2)
        xr_sl = Slot(2)

        p.o("g", "dma_start", a2sb[:], a2r_d, inc=(dC2, 16))
        p.o("g", "dma_start", b2sb[:], b2t_d, inc=(dC2, 16))
        p.w("t", dC2)

        gu_tiles = []
        for j in range(YC):
            R = 128 if j < YC - 1 else 64
            for r in range(4):
                gu_tiles.append((j, r, YF * r + 128 * j, R))

        NTl = len(gu_tiles)
        for n in range(N4):
            p.w("t", ve)
            for c, (j, r, off, R) in enumerate(gu_tiles):
                ci = n * NTl + c
                gub_sl.wait(p, "y", ci)
                gv = gub[0:R, (c % 3) * 512:(c % 3 + 1) * 512]
                p.o("y", "dma_start", gv,
                    GU[off:off + R, 512 * n:512 * (n + 1)],
                    inc=(rgF.s(ci), 16))
                p.w("t", rgF.s(ci))
                p.o("t", "matmul", psFA[0:2, :], a2sb[0:R, 2 * c:2 * (c + 1)],
                    gv, start=(c == 0), stop=(c == NTl - 1),
                    inc=(pe, 1))
                p.w("v", rgF.s(ci))
                dst = yfa[0:R, 512 * j:512 * (j + 1)]
                if r == 0:
                    p.o("v", "tensor_copy", dst, gv, inc=(ve, 1), cwait=True,
                        cinc=True)
                else:
                    p.o("v", "tensor_tensor", dst, dst, gv, ALU.add,
                        inc=(ve, 1), cwait=True, cinc=True)
                gub_sl.set(ci, ve, pe)
            p.w("v", ve)
            p.o("v", "tensor_copy", yfb[:, 0:(YC - 1) * 512],
                yfa[:, 0:(YC - 1) * 512], cwait=True, cinc=True)
            p.o("v", "tensor_copy", yfb[0:64, (YC - 1) * 512:YC * 512],
                yfa[0:64, (YC - 1) * 512:YC * 512], inc=(ve, 1), cwait=True)
            p.w("v", pe)
            p.o("v", "tensor_copy", xa2f[:], psFA[0:2, :], inc=(ve, 1))
            p.w("s", ve)
            p.o("s", "mul", xa2b[:], xa2f[:], 0.5, inc=(ac, 1))
            blk_guard = [(ac, ac.t), (ve, ve.t)]
            for m in range(DC):
                par = m % 2
                wd2_sl.wait(p, "g", m)
                wv_ = wd2[:, par * YC * 128:(par + 1) * YC * 128]
                p.o("g", "dma_start",
                    wv_[:, 0:21 * 128].rearrange("p (k m) -> p k m", k=21),
                    w2t_d[0:2688, 128 * m:128 * (m + 1)].rearrange(
                        "(k p) m -> p k m", p=128), inc=(rwF.s(m), 16))
                p.o("g", "dma_start",
                    wv_[0:64, 21 * 128:22 * 128],
                    w2t_d[2688:2752, 128 * m:128 * (m + 1)], inc=(rwF.s(m), 16))
                p.w("t", rwF.s(m))
                p.wg("t", blk_guard)
                pp = psF[m % 2]
                psF_sl.wait(p, "t", m)
                for k in range(21):
                    p.o("t", "matmul", pp[:],
                        wv_[:, 128 * k:128 * (k + 1)],
                        yfb[:, 512 * k:512 * (k + 1)],
                        start=(k == 0), stop=False)
                p.o("t", "matmul", pp[:], wv_[0:64, 21 * 128:22 * 128],
                    yfb[0:64, 21 * 512:22 * 512], start=False, stop=False)
                p.o("t", "matmul", pp[:], b2sb[0:2, 128 * m:128 * (m + 1)],
                    xa2b[0:2, :], start=False, stop=True, inc=(pe, 1))
                wd2_sl.set(m, pe)
                p.w("s", pe)
                fof_sl.wait(p, "s", m)
                fv = fof[:, (m % 2) * 512:(m % 2 + 1) * 512]
                p.o("s", "copy", fv, pp[:], inc=(ac, 1))
                psF_sl.set(m, ac)
                p.w("t", ac)
                psFT_sl.wait(p, "t", m)
                for j in range(4):
                    p.o("t", "transpose", psFT[m % 2][:, 128 * j:128 * (j + 1)],
                        fv[:, 128 * j:128 * (j + 1)], eyef[:],
                        inc=(pe, 1) if j == 3 else None)
                fof_sl.set(m, pe)
                p.w("v", pe)
                p.o("v", "tensor_copy",
                    ftok.rearrange("p (j f) -> p j f", j=4)[:, :, 128 * m:128 * (m + 1)],
                    psFT[m % 2].rearrange("p (j f) -> p j f", j=4), inc=(ve, 1))
                psFT_sl.set(m, ve)
            for j in range(4):
                ti = 4 * n + j
                xr_sl.wait(p, "y", ti)
                xv = xrF[:, (j % 2) * D:(j % 2 + 1) * D]
                p.o("y", "dma_start", xv, X2[128 * ti:128 * (ti + 1), :],
                    inc=(rxF.s(ti), 16))
                ft = ftok[:, j * D:(j + 1) * D]
                p.w("v", ve)
                p.w("s", ve)
                ln_stats_apply(ft, ft, junkF[:])
                p.w("v", ac)
                p.w("v", rxF.s(ti))
                p.o("v", "tensor_tensor", xv, xv, ft, ALU.add, inc=(ve, 1),
                    cwait=True, cinc=True)
                p.w("y", ve)
                p.o("y", "dma_start", y_out[128 * ti:128 * (ti + 1), :], xv,
                    inc=(rsY.s(ti), 16))
                xr_sl.set(ti, rsY.s(ti))
            p.w("v", ac)

        stF.close()

    for s in sems.values():
        p.w("y", s)

    with nc.Block() as block:
        @block.tensor
        def _(h):
            p.replay("t", h)

        @block.vector
        def _(h):
            p.replay("v", h)

        @block.scalar
        def _(h):
            p.replay("s", h)

        @block.gpsimd
        def _(h):
            p.replay("g", h)

        @block.sync
        def _(h):
            p.replay("y", h)

    return nc, st



def _host_prep(inputs):
    f32 = np.float32
    c = np.ascontiguousarray
    hs = np.asarray(inputs["L0_attn_hidden_states"], dtype=f32)
    common = {
        "wq": c(np.asarray(inputs["L0_attn_w_q"], f32)),
        "wk": c(np.asarray(inputs["L0_attn_w_k"], f32)),
        "wv": c(np.asarray(inputs["L0_attn_w_v"], f32)),
        "wlq": c(np.asarray(inputs["L0_attn_w_lq"], f32)),
        "wlk": c(np.asarray(inputs["L0_attn_w_lk"], f32)),
        "wlv": c(np.asarray(inputs["L0_attn_w_lv"], f32)),
        "wolo": c(np.concatenate([np.asarray(inputs["L0_attn_w_o"], f32),
                                  np.asarray(inputs["L0_attn_w_lo"], f32)], 0)),
        "w1t": c(np.asarray(inputs["L0_ffn_w1q"], f32).T),
        "w3t": c(np.asarray(inputs["L0_ffn_w3q"], f32).T),
        "w2t": c(np.asarray(inputs["L0_ffn_w2q"], f32).T),
        "a13t": c(np.concatenate([np.asarray(inputs["L0_ffn_a1"], f32),
                                  np.asarray(inputs["L0_ffn_a3"], f32)], 0).T),
        "b1t": c(np.asarray(inputs["L0_ffn_b1"], f32).T),
        "b3t": c(np.asarray(inputs["L0_ffn_b3"], f32).T),
        "a2r": _a2_retile(np.asarray(inputs["L0_ffn_a2"], f32)),
        "b2t": c(np.asarray(inputs["L0_ffn_b2"], f32).T),
        "c_eyeb": np.eye(128).astype(ml_dtypes.bfloat16),
        "c_ones": np.ones((128, 129), np.float32).astype(ml_dtypes.bfloat16),
        "c_eyef": np.eye(128).astype(f32),
        "c_mask": np.triu(np.full((128, 128), NEGINF, f32), 1),
        "c_masku": np.triu(np.full((128, 128), NEGINF, f32), 1)
        .astype(ml_dtypes.bfloat16),
        "c_negall": np.full((128, 128), NEGINF, f32).astype(ml_dtypes.bfloat16),
        "c_rope": _rope_tables(),
    }
    return hs, common


def _a2_retile(a2):
    a2t = a2.T
    out = np.zeros((128, 2 * 88), np.float32)
    t = 0
    for j in range(YC):
        R = 128 if j < YC - 1 else 64
        for r in range(4):
            off = YF * r + 128 * j
            out[0:R, 2 * t:2 * t + 2] = a2t[off:off + R]
            t += 1
    return out


def _rope_tables():
    inv = 1.0 / (10000.0 ** (np.arange(0, HD, 2, dtype=np.float64) / HD))
    t = np.arange(T, dtype=np.float64)
    fr = np.outer(t, inv)
    emb = np.concatenate([fr, fr], axis=1)
    cos = np.cos(emb).T.astype(np.float32)
    sin = np.sin(emb).T.astype(np.float32)
    ss = sin.copy()
    ss[0:64] *= -1.0
    tab = np.empty((128, 4 * T), np.float32)
    tab[:, 0 * T:1 * T] = cos * SCALE
    tab[:, 1 * T:2 * T] = ss * SCALE
    tab[:, 2 * T:3 * T] = cos
    tab[:, 3 * T:4 * T] = ss
    return tab.astype(ml_dtypes.bfloat16)


_CACHE = {}


def kernel(**inputs):
    hs, common = _host_prep(inputs)
    nc, st = build_core()
    in_maps = []
    for b in range(8):
        m = dict(common)
        m["x"] = np.ascontiguousarray(hs[b])
        in_maps.append(m)
    res = run_bass_kernel_spmd(nc, in_maps, core_ids=list(range(8)))
    out = np.stack([np.asarray(r["y"]) for r in res.results], axis=0)
    return out.astype(np.float32)
